# version 35
# baseline (speedup 1.0000x reference)
import contextlib

import numpy as np

import concourse.bass as bass
import concourse.mybir as mybir
from concourse.bass_utils import run_bass_kernel_spmd

F32 = mybir.dt.float32
I32 = mybir.dt.int32
BF16 = mybir.dt.bfloat16
AF = mybir.ActivationFunctionType
ALU = mybir.AluOpType

D = 1024
FF = 4096
INNER = 2048
R = 16
NCORES = 8
EPS = 1e-5
QMAX = 127.0

KD = D // 128
KF = FF // 128
KI = INNER // 128

SFC = D + 128
SPJ = FF + 128
KF2 = KF // 2
SPJH = KF2 * 128
QFC = D + 256
QPJB = SPJH + 256



ENGINES = ("pe", "act", "dve", "pool", "sp")
DMA_ENGS = ("sp", "act", "pool")


class Buf:
    __slots__ = ("arena", "p0", "p1", "c0", "c1")

    def __init__(self, arena, p0, p1, c0, c1):
        self.arena, self.p0, self.p1, self.c0, self.c1 = arena, p0, p1, c0, c1

    def key(self):
        return (self.p0, self.p1, self.c0, self.c1)

    def overlaps(self, o):
        return (self.p0 < o.p1 and o.p0 < self.p1
                and self.c0 < o.c1 and o.c0 < self.c1)


QPOOL = {"sp": 12, "act": 8, "pool": 8}


class Prog:
    def __init__(self):
        self.items = {e: [] for e in ENGINES}
        self.tick = {e: 0 for e in ENGINES}
        self.qcnt = {e: 0 for e in DMA_ENGS}
        self.quse = {(e, k): 0 for e in DMA_ENGS for k in range(QPOOL[e])}
        self.marks = {e: {} for e in ENGINES}
        self.recs = {}
        self.nwaits = 0

    def _arena_recs(self, arena):
        return self.recs.setdefault(arena, {})

    def _wants(self, eng, semkey, val, waits):
        if val <= 0:
            return
        if semkey == ("t", "pe") and eng == "pe":
            return
        wm = self.marks[eng].get(semkey, 0)
        if val <= wm:
            return
        if val > waits.get(semkey, 0):
            waits[semkey] = val

    def _dep_read(self, eng, buf, waits):
        for key, rec in self._arena_recs(buf.arena).items():
            if rec["w"] is None:
                continue
            if Buf(buf.arena, *key).overlaps(buf):
                self._wants(eng, rec["w"][0], rec["w"][1], waits)

    def _dep_write(self, eng, buf, waits):
        for key, rec in self._arena_recs(buf.arena).items():
            if not Buf(buf.arena, *key).overlaps(buf):
                continue
            if rec["w"] is not None:
                self._wants(eng, rec["w"][0], rec["w"][1], waits)
            for semkey, val in rec["r"].items():
                self._wants(eng, semkey, val, waits)

    def op(self, eng, fn, reads=(), writes=(), dma=False, tag=""):
        waits = {}
        for b in reads:
            self._dep_read(eng, b, waits)
        for b in writes:
            self._dep_write(eng, b, waits)
        for semkey, val in sorted(waits.items(), key=lambda kv: str(kv[0])):
            self.items[eng].append(("wait", semkey, val))
            self.nwaits += 1
            if val > self.marks[eng].get(semkey, 0):
                self.marks[eng][semkey] = val
        if dma:
            slot = self.qcnt[eng] % QPOOL[eng]
            self.qcnt[eng] += 1
            semkey = ("q", eng, slot)
            prev = 16 * self.quse[(eng, slot)]
            if prev > self.marks[eng].get(semkey, 0):
                self.items[eng].append(("wait", semkey, prev))
                self.nwaits += 1
                self.marks[eng][semkey] = prev
            self.quse[(eng, slot)] += 1
            val, amt = 16 * self.quse[(eng, slot)], 16
        else:
            if eng == "pool" and self.tick[eng] > 0:
                prev = self.tick[eng]
                if prev > self.marks[eng].get(("t", eng), 0):
                    self.items[eng].append(("wait", ("t", eng), prev))
                    self.nwaits += 1
                    self.marks[eng][("t", eng)] = prev
            self.tick[eng] += 1
            semkey, val, amt = ("t", eng), self.tick[eng], 1
        self.items[eng].append(("call", fn, semkey, amt, tag))
        done = (semkey, val)
        for b in reads:
            rec = self._arena_recs(b.arena).setdefault(
                b.key(), {"w": None, "r": {}})
            if val > rec["r"].get(semkey, 0):
                rec["r"][semkey] = val
        for b in writes:
            rec = self._arena_recs(b.arena).setdefault(
                b.key(), {"w": None, "r": {}})
            rec["w"] = done
            rec["r"] = {}
        return done

    def wait_all_end(self):
        for e in DMA_ENGS:
            for k in range(QPOOL[e]):
                v = 16 * self.quse[(e, k)]
                if v > self.marks[e].get(("q", e, k), 0):
                    self.items[e].append(("wait", ("q", e, k), v))

    def replay_fns(self, sems):
        def mk(eng):
            def body(engobj):
                for it in self.items[eng]:
                    if it[0] == "wait":
                        engobj.wait_ge(sems[it[1]], it[2])
                    else:
                        inst = it[1](engobj)
                        inst.then_inc(sems[it[2]], it[3])
            return body
        return {e: mk(e) for e in ENGINES}



def build_program(tok, chunk):
    nch = tok // chunk
    nc = bass.Bass("TRN2", target_bir_lowering=False)

    def din(name, shape):
        return nc.dram_tensor(name, shape, F32, kind="ExternalInput").ap()

    FCL = (KF + NCORES - 1) // NCORES
    PJL = (KD + NCORES - 1) // NCORES
    WINL = (2 * KI + NCORES - 1) // NCORES
    WDTL = (KI + NCORES - 1) // NCORES
    WOUTL = (KD + NCORES - 1) // NCORES
    x_in = din("x_fm", [D, tok])
    wfc_l = din("wfc_l", [FCL, 128, SFC])
    wpjA_l = din("wpjA_l", [PJL, 128, SPJH])
    wpjB_l = din("wpjB_l", [PJL, 128, SPJ - SPJH])
    win_f = din("win_f", [2 * KI, 128, D])
    wdt_l = din("wdt_l", [WDTL, 128, INNER])
    wout_l = din("wout_l", [WOUTL, 128, INNER])
    afc_in = din("afc", [128, KD * R])
    apj_in = din("apj", [128, KF * R])
    biasfc_in = din("bias_fc", [128, KF])
    biaspj_in = din("bias_pj", [128, KD])
    bdt_in = din("bias_dt", [128, KI])
    wnorm_in = din("wnorm", [128, KD])
    ident_in = din("ident", [128, 128])
    cmask_in = din("cmask", [128, NCORES])

    out_fm = nc.dram_tensor("out_fm", [D, tok], F32, kind="ExternalOutput").ap()

    FCC = FCL * QFC
    RO_PJA = 0
    RO_PJB = RO_PJA + PJL * SPJH
    PJC = RO_PJB + PJL * QPJB
    RO_WDT = 0
    RO_WOUT = RO_WDT + WDTL * 2 * INNER
    MXCB = RO_WOUT + WOUTL * 2 * INNER
    prep_fc = nc.dram_tensor("prep_fc", [128, FCC], BF16).ap()
    prep_fc_all = nc.dram_tensor("prep_fc_all", [NCORES, 128, FCC],
                                 BF16).ap()
    prep_pj = nc.dram_tensor("prep_pj", [128, PJC], BF16).ap()
    prep_pj_all = nc.dram_tensor("prep_pj_all", [NCORES, 128, PJC],
                                 BF16).ap()
    win_cache = nc.dram_tensor("win_cache", [128, 2 * KI * 2 * D],
                               BF16).ap()
    prep_mxB = nc.dram_tensor("prep_mxB", [128, MXCB], BF16).ap()
    prep_mxB_all = nc.dram_tensor("prep_mxB_all", [NCORES, 128, MXCB],
                                  BF16).ap()
    mxin = nc.dram_tensor("mxin", [128, 2], F32).ap()
    mxout = nc.dram_tensor("mxout", [128, 2], F32).ap()
    sv_spill = nc.dram_tensor("sv_spill", [nch, 128, KI * chunk], F32).ap()
    g_spill = nc.dram_tensor("g_spill", [nch, 128, KI * chunk], F32).ap()
    x1_spill = nc.dram_tensor("x1_spill", [nch, 128, KD * chunk], F32).ap()
    xin_spill = nc.dram_tensor("xin_spill", [nch, 128, KD * chunk], F32).ap()
    agin = nc.dram_tensor("agin", [2, 128, KI], F32).ap()
    agout = nc.dram_tensor("agout", [NCORES, 2, 128, KI], F32).ap()

    sb = {}

    def arena(name, parts, cols, dt=F32):
        h = nc.alloc_sbuf_tensor(name, [parts, cols], dt)
        sb[name] = h.ap()
        return h

    BIGN = max(KF, 2 * KI) * chunk
    arena("A_x", 128, KD * chunk)
    h_xn = arena("A_xn", 128, KD * chunk)
    arena("A_xin", 128, KD * chunk)
    h_big = arena("A_big", 128, BIGN)
    h_lam = arena("A_lam", 128, KI * chunk)
    arena("A_scr", 128, 4 * chunk)
    arena("A_st", 128, 8 * chunk)
    h_wa = arena("A_wa", 128, 2304)
    h_wb = arena("A_wb", 128, 2304)
    arena("A_qi", 128, min(1024, 2 * chunk), I32)
    h_tt = arena("A_tt", 128, 2 * chunk)
    NLA = (KD + KF) * R
    h_lp = arena("A_lp", 128, NLA)
    arena("A_g1", 1, 144)

    def alias(name, base_h, byte_off, parts, elems, dt):
        addr = nc.lookup_mloc(base_h).addr + byte_off
        sb[name] = nc.alloc_sbuf_tensor_at(
            name, [parts, elems], dt, offset=addr).ap()

    alias("A_xnh", h_xn, 0, 128, KD * chunk, BF16)
    alias("A_xnl", h_xn, 2 * KD * chunk, 128, KD * chunk, BF16)
    alias("A_bgh", h_big, 0, 128, BIGN, BF16)
    alias("A_bgl", h_big, 2 * KF * chunk, 128, KF * chunk, BF16)
    alias("A_wah", h_wa, 0, 128, 4608, BF16)
    alias("A_wbh", h_wb, 0, 128, 4608, BF16)
    assert nc.lookup_mloc("A_wb").addr == nc.lookup_mloc("A_wa").addr + 9216
    PJT = SPJH + QPJB
    assert 4 * QFC * 2 <= 18432 and 2 * PJT * 2 <= 18432
    alias("A_wq", h_wa, 0, 128, 4 * QFC, BF16)
    alias("A_wp", h_wa, 0, 128, 2 * PJT, BF16)
    alias("A_tth", h_tt, 0, 128, 2 * chunk, BF16)
    alias("A_ttl", h_tt, 4 * chunk, 128, 2 * chunk, BF16)
    alias("A_lmh", h_lam, 0, 128, 2 * KI * chunk, BF16)
    alias("A_lph", h_lp, 0, 128, NLA, BF16)
    alias("A_lpl", h_lp, 2 * NLA, 128, NLA, BF16)

    ALIAS = {"A_xnh": ("A_xn", 0), "A_xnl": ("A_xn", 2 * KD * chunk),
             "A_lmh": ("A_lam", 0),
             "A_bgh": ("A_big", 0), "A_bgl": ("A_big", 2 * KF * chunk),
             "A_wah": ("A_wa", 0), "A_wbh": ("A_wa", 9216),
             "A_wq": ("A_wa", 0), "A_wp": ("A_wa", 0),
             "A_tth": ("A_tt", 0), "A_ttl": ("A_tt", 4 * chunk),
             "A_lph": ("A_lp", 0), "A_lpl": ("A_lp", 2 * NLA)}

    C_AFC = 0
    C_APJ = C_AFC + KD * R
    C_BIASFC = C_APJ + KF * R
    C_BIASPJ = C_BIASFC + KF
    C_BDT = C_BIASPJ + KD
    C_WNORM = C_BDT + KI
    C_IDENT = C_WNORM + KD
    C_CMASK = C_IDENT + 128
    C_ONES = C_CMASK + NCORES
    C_ONESC = C_ONES + 128
    C_SCAL = C_ONESC + chunk
    C_MST = C_SCAL + 16
    C_SLAST = C_MST + 56
    C_RLAST = C_SLAST + KI
    C_CARRY = C_RLAST + KI
    C_AG = C_CARRY + KI
    C_BG = C_AG + NCORES * KI
    C_CAR = C_BG + NCORES * KI
    C_BPH = C_CAR + 3 * KI
    C_EPS = C_BPH + KD
    C_INVD = C_EPS + 1
    C_END = C_INVD + 1
    arena("A_c", 128, C_END)

    ps = [nc.alloc_psum_tensor(f"ps{i}", [128, 512], F32).ap()
          for i in range(8)]
    PS_MAIN = (0, 1, 2, 7)
    PS_S1, PS_S2, PS_T, PS_AUX = 3, 4, 5, 6

    p = Prog()

    def Bf(name, p0, p1, e0, e1):
        if name == "A_wb":
            return Buf("A_wa", p0, p1, 9216 + 4 * e0, 9216 + 4 * e1)
        return Buf(name, p0, p1, 4 * e0, 4 * e1)

    def BC(c0, n):
        return Buf("A_c", 0, 128, 4 * c0, 4 * (c0 + n))

    def BT(arena_name, t, width=None):
        w = chunk if width is None else width
        return Buf(arena_name, 0, 128, 4 * t * w, 4 * (t + 1) * w)

    def BA(alias_name, p0, p1, e0, e1):
        base, off = ALIAS[alias_name]
        return Buf(base, p0, p1, off + 2 * e0, off + 2 * e1)

    def BAT(alias_name, t, width=None):
        w = chunk if width is None else width
        return BA(alias_name, 0, 128, t * w, (t + 1) * w)

    def BD(name, lo, hi):
        return Buf("dram:" + name, 0, 1, lo, hi)

    def BP(i):
        return Buf("psum", 0, 128, i * 512, (i + 1) * 512)

    def dma(eng, out_ap, in_ap, reads, writes, tag=""):
        def fn(e, out_ap=out_ap, in_ap=in_ap):
            return e.dma_start(out=out_ap, in_=in_ap)
        p.op(eng, fn, reads=reads, writes=writes, dma=True, tag=tag)

    def act(fn_builder, reads, writes, tag=""):
        p.op("act", fn_builder, reads=reads, writes=writes, tag=tag)

    def dve(fn_builder, reads, writes, tag=""):
        p.op("dve", fn_builder, reads=reads, writes=writes, tag=tag)

    def pe(fn_builder, reads, writes, tag=""):
        p.op("pe", fn_builder, reads=reads, writes=writes, tag=tag)

    def pool(fn_builder, reads, writes, tag=""):
        p.op("pool", fn_builder, reads=reads, writes=writes, tag=tag)

    for (c0, n, src) in ((C_AFC, KD * R, afc_in), (C_APJ, KF * R, apj_in),
                         (C_BIASFC, KF, biasfc_in), (C_BIASPJ, KD, biaspj_in),
                         (C_BDT, KI, bdt_in), (C_WNORM, KD, wnorm_in),
                         (C_IDENT, 128, ident_in), (C_CMASK, NCORES, cmask_in)):
        dma("act", sb["A_c"][:, c0:c0 + n], src[:, :], [], [BC(c0, n)])

    dve(lambda e: e.memset(sb["A_c"][:, C_ONES:C_ONES + 128], 1.0),
        [], [BC(C_ONES, 128)], "ones")
    dve(lambda e: e.memset(sb["A_c"][:, C_ONESC:C_ONESC + chunk], 1.0),
        [], [BC(C_ONESC, chunk)], "onesc")
    dve(lambda e: e.memset(sb["A_c"][:, C_SLAST:C_SLAST + KI], 0.0),
        [], [BC(C_SLAST, KI)], "slast0")
    dve(lambda e: e.memset(sb["A_c"][:, C_RLAST:C_RLAST + KI], 1.0),
        [], [BC(C_RLAST, KI)], "rlast1")
    dve(lambda e: e.memset(sb["A_tth"][:, :], 0.0),
        [], [BA("A_tth", 0, 128, 0, 2 * chunk)], "tth0")
    dve(lambda e: e.memset(sb["A_ttl"][:, :], 0.0),
        [], [BA("A_ttl", 0, 128, 0, 2 * chunk)], "ttl0")
    dve(lambda e: e.memset(sb["A_c"][:, C_EPS:C_EPS + 1], EPS),
        [], [BC(C_EPS, 1)], "eps")
    dve(lambda e: e.memset(sb["A_c"][:, C_INVD:C_INVD + 1], 1.0 / D),
        [], [BC(C_INVD, 1)], "invd")
    dve(lambda e: e.tensor_scalar_mul(
        sb["A_c"][:, C_BPH:C_BPH + KD],
        sb["A_c"][:, C_BIASPJ:C_BIASPJ + KD], 0.5),
        [BC(C_BIASPJ, KD)], [BC(C_BPH, KD)], "bph")
    act(lambda e: e.activation(out=sb["A_lph"][:, 0:NLA],
                               in_=sb["A_c"][:, C_AFC:C_AFC + NLA],
                               func=AF.Copy),
        [BC(C_AFC, NLA)], [BA("A_lph", 0, 128, 0, NLA)], "lah")
    dve(lambda e: e.tensor_sub(sb["A_lpl"][:, 0:NLA],
                               sb["A_c"][:, C_AFC:C_AFC + NLA],
                               sb["A_lph"][:, 0:NLA]),
        [BC(C_AFC, NLA), BA("A_lph", 0, 128, 0, NLA)],
        [BA("A_lpl", 0, 128, 0, NLA)], "lal")

    MXW = max(D, SPJH, FF - SPJH)
    MXN = max(1, min(3, BIGN // MXW))

    def finish_scale(mxcol_rel, scal_rs, extra, tag):
        g1 = sb["A_g1"]
        mxcol = C_MST + mxcol_rel
        pe(lambda e: e.transpose(
            ps[PS_AUX][0:1, 0:128], sb["A_c"][:, mxcol:mxcol + 1],
            sb["A_c"][:, C_IDENT:C_IDENT + 128]),
           [BC(mxcol, 1), BC(C_IDENT, 128)], [BP(PS_AUX)], f"tr_{tag}")
        act(lambda e: e.activation(out=g1[0:1, 0:128],
                                   in_=ps[PS_AUX][0:1, 0:128], func=AF.Copy),
            [BP(PS_AUX)], [Bf("A_g1", 0, 1, 0, 128)], f"trc_{tag}")
        dve(lambda e: e.reduce_max(g1[0:1, 128:129], g1[0:1, 0:128],
                                   axis=mybir.AxisListType.X),
            [Bf("A_g1", 0, 1, 0, 128)], [Bf("A_g1", 0, 1, 128, 129)],
            f"gm_{tag}")
        dve(lambda e: e.reciprocal(g1[0:1, 129:130], g1[0:1, 128:129]),
            [Bf("A_g1", 0, 1, 128, 129)], [Bf("A_g1", 0, 1, 129, 130)],
            f"rg_{tag}")
        dve(lambda e: e.tensor_scalar_mul(g1[0:1, 130:131],
                                          g1[0:1, 129:130], QMAX),
            [Bf("A_g1", 0, 1, 129, 130)], [Bf("A_g1", 0, 1, 130, 131)],
            f"rs_{tag}")
        dve(lambda e: e.tensor_scalar_mul(g1[0:1, 131:132],
                                          g1[0:1, 128:129], 1.0 / QMAX),
            [Bf("A_g1", 0, 1, 128, 129)], [Bf("A_g1", 0, 1, 131, 132)],
            f"s_{tag}")
        pe(lambda e: e.matmul(ps[PS_AUX][:, 0:2],
                              sb["A_c"][0:1, C_ONES:C_ONES + 128],
                              g1[0:1, 130:132], start=True, stop=True),
           [BC(C_ONES, 128), Bf("A_g1", 0, 1, 130, 132)],
           [BP(PS_AUX)], f"bc_{tag}")
        act(lambda e: e.activation(
            out=sb["A_c"][:, C_SCAL + scal_rs:C_SCAL + scal_rs + 2],
            in_=ps[PS_AUX][:, 0:2], func=AF.Copy),
            [BP(PS_AUX)], [BC(C_SCAL + scal_rs, 2)], f"bcc_{tag}")
        dve(lambda e: e.tensor_scalar_mul(
            sb["A_c"][:, C_SCAL + extra:C_SCAL + extra + 1],
            sb["A_c"][:, C_SCAL + scal_rs:C_SCAL + scal_rs + 1], 2.0),
            [BC(C_SCAL + scal_rs, 1)], [BC(C_SCAL + extra, 1)],
            f"x2_{tag}")

    QU = min(512, chunk)

    def quant_to_bf(wb, wbh, wcols, scal_rs, tag):
        rs_ap = sb["A_c"][:, C_SCAL + scal_rs:C_SCAL + scal_rs + 1]
        rs_buf = BC(C_SCAL + scal_rs, 1)
        QS = ((0, chunk, 3 * chunk), (4 * chunk, 6 * chunk, 7 * chunk))
        for o in range(0, wcols, QU):
            L = min(QU, wcols - o)
            s0, s1_, s2_ = QS[(o // QU) % 2]
            qbase = ((o // QU) % 2) * QU
            wsl = sb[wb][:, o:o + L]
            wreg = Bf(wb, 0, 128, o, o + L)
            qi = sb["A_qi"][:, qbase:qbase + L]
            qireg = Bf("A_qi", 0, 128, qbase, qbase + L)
            r0 = sb["A_st"][:, s0:s0 + L]
            r0reg = Bf("A_st", 0, 128, s0, s0 + L)
            fr = sb["A_st"][:, s1_:s1_ + L]
            frreg = Bf("A_st", 0, 128, s1_, s1_ + L)
            g2 = sb["A_st"][:, s2_:s2_ + L]
            g2reg = Bf("A_st", 0, 128, s2_, s2_ + L)
            qo = sb[wbh][:, o:o + L]
            qoreg = BA(wbh, 0, 128, o, o + L)
            act(lambda e, wsl=wsl, qi=qi: e.activation(
                out=qi, in_=wsl, func=AF.Copy, scale=rs_ap),
                [wreg, rs_buf], [qireg], f"qc_{tag}")
            act(lambda e, qi=qi, r0=r0: e.activation(
                out=r0, in_=qi, func=AF.Copy),
                [qireg], [r0reg], f"qr_{tag}")
            dve(lambda e, wsl=wsl, r0=r0, fr=fr:
                e.scalar_tensor_tensor(out=fr, in0=wsl, scalar=rs_ap,
                                       in1=r0, op0=ALU.mult,
                                       op1=ALU.subtract),
                [wreg, r0reg, rs_buf], [frreg], f"qf_{tag}")
            dve(lambda e, fr=fr, r0=r0, g2=g2: e.scalar_tensor_tensor(
                out=g2, in0=fr, scalar=-0.5, in1=r0,
                op0=ALU.is_lt, op1=ALU.subtract),
                [frreg, r0reg], [g2reg], f"qg_{tag}")
            dve(lambda e, fr=fr, g2=g2, r0=r0: e.scalar_tensor_tensor(
                out=r0, in0=fr, scalar=0.5, in1=g2,
                op0=ALU.is_gt, op1=ALU.subtract),
                [frreg, g2reg], [r0reg], f"qa_{tag}")
            dve(lambda e, r0=r0, qo=qo: e.tensor_scalar(
                out=qo, in0=r0, scalar1=QMAX, scalar2=-QMAX - 1.0,
                op0=ALU.min, op1=ALU.max),
                [r0reg], [qoreg], f"qk_{tag}")

    def lora_split(wb, wbh, raw_e0, dst_e0, tag):
        raw = sb[wb][:, raw_e0:raw_e0 + 128]
        rreg = Bf(wb, 0, 128, raw_e0, raw_e0 + 128)
        hi = sb[wbh][:, dst_e0:dst_e0 + 128]
        hireg = BA(wbh, 0, 128, dst_e0, dst_e0 + 128)
        lo = sb[wbh][:, dst_e0 + 128:dst_e0 + 256]
        loreg = BA(wbh, 0, 128, dst_e0 + 128, dst_e0 + 256)
        act(lambda e: e.activation(out=hi, in_=raw, func=AF.Copy),
            [rreg], [hireg], f"lh_{tag}")
        dve(lambda e: e.tensor_sub(lo, raw, hi),
            [rreg, hireg], [loreg], f"ll_{tag}")

    ST_A, ST_MSQ, ST_T1, ST_RSTD, ST_RMS, ST_T2 = (
        0, chunk, 2 * chunk, 3 * chunk, 4 * chunk, 5 * chunk)

    def st(off):
        return sb["A_st"][:, off:off + chunk]

    def streg(off):
        return Bf("A_st", 0, 128, off, off + chunk)

    def ln_stats(x_arena, with_rms, tag):
        ones_l = sb["A_c"][:, C_ONES:C_ONES + 128]
        xt = [sb[x_arena][:, d * chunk:(d + 1) * chunk] for d in range(KD)]
        xreg = [BT(x_arena, d) for d in range(KD)]

        def s1fn(e):
            last = None
            for d in range(KD):
                last = e.matmul(ps[PS_S1][:, 0:chunk], ones_l, xt[d],
                                start=(d == 0), stop=(d == KD - 1))
            return last
        pe(s1fn, [BC(C_ONES, 128)] + xreg, [BP(PS_S1)], f"s1_{tag}")
        kdh = KD // 2
        for half in range(2):
            for i in range(kdh):
                d = half * kdh + i
                act(lambda e, d=d, i=i: e.activation(
                    out=sb["A_st"][:, i * chunk:(i + 1) * chunk],
                    in_=xt[d], func=AF.Square),
                    [xreg[d]], [streg(i * chunk)], f"sq_{tag}{d}")

            def s2fn(e, half=half):
                last = None
                for i in range(kdh):
                    last = e.matmul(
                        ps[PS_S2][:, 0:chunk], ones_l,
                        sb["A_st"][:, i * chunk:(i + 1) * chunk],
                        start=(half == 0 and i == 0),
                        stop=(half == 1 and i == kdh - 1))
                return last
            pe(s2fn, [BC(C_ONES, 128)] + [streg(i * chunk)
                                          for i in range(kdh)],
               [BP(PS_S2)], f"s2_{tag}{half}")
        ln_stats_chain(with_rms, tag)

    def ln_stats_chain(with_rms, tag):
        act(lambda e: e.activation(out=st(ST_A), in_=ps[PS_S1][:, 0:chunk],
                                   func=AF.Copy,
                                   scale=sb["A_c"][:, C_INVD:C_INVD + 1]),
            [BP(PS_S1), BC(C_INVD, 1)], [streg(ST_A)], f"ma_{tag}")
        act(lambda e: e.activation(out=st(ST_MSQ), in_=ps[PS_S2][:, 0:chunk],
                                   func=AF.Copy,
                                   scale=sb["A_c"][:, C_INVD:C_INVD + 1]),
            [BP(PS_S2), BC(C_INVD, 1)], [streg(ST_MSQ)], f"mq_{tag}")
        act(lambda e: e.activation(out=st(ST_T1), in_=st(ST_A),
                                   func=AF.Square),
            [streg(ST_A)], [streg(ST_T1)], f"a2_{tag}")
        dve(lambda e: e.tensor_sub(st(ST_MSQ), st(ST_MSQ), st(ST_T1)),
            [streg(ST_MSQ), streg(ST_T1)], [streg(ST_MSQ)], f"vr_{tag}")
        act(lambda e: e.activation(out=st(ST_T1), in_=st(ST_MSQ),
                                   func=AF.Ln,
                                   bias=sb["A_c"][:, C_EPS:C_EPS + 1]),
            [streg(ST_MSQ), BC(C_EPS, 1)], [streg(ST_T1)], f"sg_{tag}")
        act(lambda e: e.activation(out=st(ST_RSTD), in_=st(ST_T1),
                                   func=AF.Exp, scale=-0.5),
            [streg(ST_T1)], [streg(ST_RSTD)], f"rs_{tag}")
        if with_rms:
            dve(lambda e: e.tensor_mul(st(ST_T1), st(ST_MSQ), st(ST_RSTD)),
                [streg(ST_MSQ), streg(ST_RSTD)], [streg(ST_T1)],
                f"r1_{tag}")
            dve(lambda e: e.tensor_mul(st(ST_T1), st(ST_T1), st(ST_RSTD)),
                [streg(ST_T1), streg(ST_RSTD)], [streg(ST_T1)],
                f"r2_{tag}")
            act(lambda e: e.activation(out=st(ST_T2), in_=st(ST_T1),
                                       func=AF.Ln,
                                       bias=sb["A_c"][:, C_EPS:C_EPS + 1]),
                [streg(ST_T1), BC(C_EPS, 1)], [streg(ST_T2)], f"r3_{tag}")
            act(lambda e: e.activation(out=st(ST_RMS), in_=st(ST_T2),
                                       func=AF.Exp, scale=-0.5),
                [streg(ST_T2)], [streg(ST_RMS)], f"r4_{tag}")

    def ln_apply(x_arena, out_arena, tag):
        for d in range(KD):
            xs = sb[x_arena][:, d * chunk:(d + 1) * chunk]
            os_ = sb[out_arena][:, d * chunk:(d + 1) * chunk]
            dve(lambda e, xs=xs, os_=os_: e.tensor_sub(os_, xs, st(ST_A)),
                [BT(x_arena, d), streg(ST_A)], [BT(out_arena, d)],
                f"la_{tag}{d}")
            dve(lambda e, os_=os_: e.tensor_mul(os_, os_, st(ST_RSTD)),
                [BT(out_arena, d), streg(ST_RSTD)], [BT(out_arena, d)],
                f"lb_{tag}{d}")

    def ln_apply_pair(x_arena, tag):
        for d in range(KD):
            xs = sb[x_arena][:, d * chunk:(d + 1) * chunk]
            t1 = sb["A_scr"][:, (d % 2) * chunk:(d % 2 + 1) * chunk]
            t1reg = BT("A_scr", d % 2)
            t2 = sb["A_scr"][:, (2 + d % 2) * chunk:(3 + d % 2) * chunk]
            t2reg = BT("A_scr", 2 + d % 2)
            hi = sb["A_xnh"][:, d * chunk:(d + 1) * chunk]
            hireg = BAT("A_xnh", d)
            lo = sb["A_xnl"][:, d * chunk:(d + 1) * chunk]
            loreg = BAT("A_xnl", d)
            pool(lambda e, xs=xs, t1=t1: e.tensor_sub(t1, xs, st(ST_A)),
                 [BT(x_arena, d), streg(ST_A)], [t1reg], f"pa_{tag}{d}")
            dve(lambda e, t1=t1, t2=t2: e.tensor_mul(t2, t1, st(ST_RSTD)),
                [t1reg, streg(ST_RSTD)], [t2reg], f"pb_{tag}{d}")
            act(lambda e, t2=t2, hi=hi: e.activation(
                out=hi, in_=t2, func=AF.Copy),
                [t2reg], [hireg], f"pc_{tag}{d}")
            dve(lambda e, t2=t2, hi=hi, lo=lo: e.tensor_sub(lo, t2, hi),
                [t2reg, hireg], [loreg], f"pd_{tag}{d}")

    def lp_h(e0, n):
        return sb["A_lph"][:, e0:e0 + n]

    def lp_l(e0, n):
        return sb["A_lpl"][:, e0:e0 + n]

    def ffn(out_evac, tag, bg=None, after_fc=None, stats_x=None):
        xnh = [sb["A_xnh"][:, k * chunk:(k + 1) * chunk] for k in range(KD)]
        xnl = [sb["A_xnl"][:, k * chunk:(k + 1) * chunk] for k in range(KD)]
        xnhreg = [BAT("A_xnh", k) for k in range(KD)]
        xnlreg = [BAT("A_xnl", k) for k in range(KD)]
        lpbufs = [BA("A_lph", 0, 128, 0, NLA), BA("A_lpl", 0, 128, 0, NLA)]

        for k in range(KD):
            def tfk(e, k=k):
                ah = lp_h(k * R, R)
                al = lp_l(k * R, R)
                e.matmul(ps[PS_T][0:R, 0:chunk], ah, xnh[k],
                         start=(k == 0), stop=False)
                e.matmul(ps[PS_T][0:R, 0:chunk], ah, xnl[k],
                         start=False, stop=False)
                return e.matmul(ps[PS_T][0:R, 0:chunk], al, xnh[k],
                                start=False, stop=(k == KD - 1))
            pe(tfk, lpbufs + [xnhreg[k], xnlreg[k]], [BP(PS_T)],
               f"tf_{tag}{k}")
        ttmp = sb["A_st"][0:R, ST_T2:ST_T2 + chunk]
        ttmpreg = Bf("A_st", 0, R, ST_T2, ST_T2 + chunk)
        act(lambda e: e.activation(
            out=ttmp, in_=ps[PS_T][0:R, 0:chunk], func=AF.Copy,
            scale=sb["A_c"][0:R, C_SCAL + 4:C_SCAL + 5]),
            [BP(PS_T), BC(C_SCAL + 4, 1)], [ttmpreg], f"te_{tag}")
        act(lambda e: e.activation(
            out=sb["A_tth"][0:R, 0:chunk], in_=ttmp, func=AF.Copy),
            [ttmpreg], [BA("A_tth", 0, R, 0, chunk)], f"th_{tag}")
        dve(lambda e: e.tensor_sub(
            sb["A_ttl"][0:R, 0:chunk], ttmp, sb["A_tth"][0:R, 0:chunk]),
            [ttmpreg, BA("A_tth", 0, R, 0, chunk)],
            [BA("A_ttl", 0, R, 0, chunk)], f"tl_{tag}")

        for m in range(KF):
            tb = (m % 4) * QFC
            qreg = BA("A_wq", 0, 128, tb, tb + QFC)
            dma("sp", sb["A_wq"][:, tb:tb + QFC],
                prep_fc_all[m % NCORES][:, (m // NCORES) * QFC:
                                        (m // NCORES + 1) * QFC],
                [BD("agfc", 0, 1)], [qreg],
                f"lfq{m}")
            bank = PS_MAIN[m % len(PS_MAIN)]

            def mmfn(e, tb=tb, bank=bank):
                last = None
                for k in range(KD):
                    qt = sb["A_wq"][:, tb + k * 128:tb + (k + 1) * 128]
                    last = e.matmul(ps[bank][:, 0:chunk], qt, xnh[k],
                                    start=(k == 0), stop=False)
                    last = e.matmul(ps[bank][:, 0:chunk], qt, xnl[k],
                                    start=False, stop=False)
                lh = sb["A_wq"][:, tb + D:tb + D + 128]
                ll = sb["A_wq"][:, tb + D + 128:tb + D + 256]
                e.matmul(ps[bank][:, 0:chunk], lh, sb["A_tth"][:, 0:chunk],
                         start=False, stop=False)
                e.matmul(ps[bank][:, 0:chunk], lh, sb["A_ttl"][:, 0:chunk],
                         start=False, stop=False)
                last = e.matmul(ps[bank][:, 0:chunk], ll,
                                sb["A_tth"][:, 0:chunk],
                                start=False, stop=True)
                return last
            pe(mmfn,
               [qreg, Bf("A_tt", 0, 128, 0, 2 * chunk)]
               + xnhreg + xnlreg,
               [BP(bank)], f"fc_{tag}{m}")
            if bg:
                bg.pop(0)()
            tsl = ST_T1 if m % 2 == 0 else ST_T2
            act(lambda e, bank=bank, m=m, tsl=tsl: e.activation(
                out=st(tsl), in_=ps[bank][:, 0:chunk], func=AF.Gelu,
                scale=sb["A_c"][:, C_SCAL + 1:C_SCAL + 2],
                bias=sb["A_c"][:, C_BIASFC + m:C_BIASFC + m + 1]),
                [BP(bank), BC(C_SCAL + 1, 1), BC(C_BIASFC + m, 1)],
                [streg(tsl)], f"fe_{tag}{m}")
            act(lambda e, m=m, tsl=tsl: e.activation(
                out=sb["A_bgh"][:, m * chunk:(m + 1) * chunk],
                in_=st(tsl), func=AF.Copy),
                [streg(tsl)], [BAT("A_bgh", m)], f"fh_{tag}{m}")
            pool(lambda e, m=m, tsl=tsl: e.tensor_sub(
                sb["A_bgl"][:, m * chunk:(m + 1) * chunk], st(tsl),
                sb["A_bgh"][:, m * chunk:(m + 1) * chunk]),
                 [streg(tsl), BAT("A_bgh", m)], [BAT("A_bgl", m)],
                 f"fl_{tag}{m}")

        if bg:
            while bg:
                bg.pop(0)()
        if after_fc is not None:
            after_fc()
        h1h = [sb["A_bgh"][:, k * chunk:(k + 1) * chunk] for k in range(KF)]
        h1l = [sb["A_bgl"][:, k * chunk:(k + 1) * chunk] for k in range(KF)]
        h1hreg = [BAT("A_bgh", k) for k in range(KF)]
        h1lreg = [BAT("A_bgl", k) for k in range(KF)]

        def t2fn(e):
            last = None
            for k in range(KF):
                ah = lp_h(KD * R + k * R, R)
                al = lp_l(KD * R + k * R, R)
                last = e.matmul(ps[PS_T][0:R, 0:chunk], ah, h1h[k],
                                start=(k == 0), stop=False)
                last = e.matmul(ps[PS_T][0:R, 0:chunk], ah, h1l[k],
                                start=False, stop=False)
                last = e.matmul(ps[PS_T][0:R, 0:chunk], al, h1h[k],
                                start=False, stop=(k == KF - 1))
            return last
        pe(t2fn, lpbufs + h1hreg + h1lreg, [BP(PS_T)], f"t2_{tag}")
        act(lambda e: e.activation(
            out=ttmp, in_=ps[PS_T][0:R, 0:chunk], func=AF.Copy,
            scale=sb["A_c"][0:R, C_SCAL + 5:C_SCAL + 6]),
            [BP(PS_T), BC(C_SCAL + 5, 1)], [ttmpreg], f"t2e_{tag}")
        act(lambda e: e.activation(
            out=sb["A_tth"][0:R, chunk:2 * chunk], in_=ttmp, func=AF.Copy),
            [ttmpreg], [BA("A_tth", 0, R, chunk, 2 * chunk)], f"t2h_{tag}")
        dve(lambda e: e.tensor_sub(
            sb["A_ttl"][0:R, chunk:2 * chunk], ttmp,
            sb["A_tth"][0:R, chunk:2 * chunk]),
            [ttmpreg, BA("A_tth", 0, R, chunk, 2 * chunk)],
            [BA("A_ttl", 0, R, chunk, 2 * chunk)], f"t2l_{tag}")

        def stats_d(t):
            ones_l = sb["A_c"][:, C_ONES:C_ONES + 128]
            xt_t = sb[stats_x][:, t * chunk:(t + 1) * chunk]
            xr = BT(stats_x, t)
            sq = sb["A_st"][:, (t % 2) * chunk:(t % 2 + 1) * chunk]
            sqr = streg((t % 2) * chunk)
            act(lambda e, xt_t=xt_t, sq=sq: e.activation(
                out=sq, in_=xt_t, func=AF.Square),
                [xr], [sqr], f"sq_{tag}{t}")

            def stfn(e, t=t, xt_t=xt_t, sq=sq):
                e.matmul(ps[PS_S1][:, 0:chunk], ones_l, xt_t,
                         start=(t == 0), stop=(t == KD - 1))
                return e.matmul(ps[PS_S2][:, 0:chunk], ones_l, sq,
                                start=(t == 0), stop=(t == KD - 1))
            pe(stfn, [BC(C_ONES, 128), xr, sqr],
               [BP(PS_S1), BP(PS_S2)], f"st_{tag}{t}")

        for d in range(KD):
            pb_ = (d % 2) * PJT
            aregA = BA("A_wp", 0, 128, pb_, pb_ + SPJH)
            aregB = BA("A_wp", 0, 128, pb_ + SPJH, pb_ + PJT)
            dma("sp", sb["A_wp"][:, pb_:pb_ + SPJH],
                prep_pj_all[d % NCORES][:, RO_PJA + (d // NCORES) * SPJH:
                                        RO_PJA + (d // NCORES + 1) * SPJH],
                [BD("agpj", 0, 1)],
                [aregA], f"lqA{d}")
            dma("sp", sb["A_wp"][:, pb_ + SPJH:pb_ + PJT],
                prep_pj_all[d % NCORES][:, RO_PJB + (d // NCORES) * QPJB:
                                        RO_PJB + (d // NCORES + 1) * QPJB],
                [BD("agpj", 0, 1)],
                [aregB], f"lqB{d}")
            bank = PS_MAIN[d % len(PS_MAIN)]

            def pjA(e, bank=bank, pb_=pb_):
                last = None
                for k in range(KF2):
                    qt = sb["A_wp"][:, pb_ + k * 128:pb_ + (k + 1) * 128]
                    last = e.matmul(ps[bank][:, 0:chunk], qt, h1h[k],
                                    start=(k == 0), stop=False)
                    last = e.matmul(ps[bank][:, 0:chunk], qt, h1l[k],
                                    start=False, stop=False)
                return last
            pe(pjA, [aregA]
               + h1hreg[:KF2] + h1lreg[:KF2],
               [BP(bank)], f"pA_{tag}{d}")

            def pjB(e, bank=bank, pb_=pb_):
                last = None
                b0 = pb_ + SPJH
                for k in range(KF2):
                    qt = sb["A_wp"][:, b0 + k * 128:b0 + (k + 1) * 128]
                    last = e.matmul(ps[bank][:, 0:chunk], qt, h1h[KF2 + k],
                                    start=False, stop=False)
                    last = e.matmul(ps[bank][:, 0:chunk], qt, h1l[KF2 + k],
                                    start=False, stop=False)
                lh = sb["A_wp"][:, b0 + SPJH:b0 + SPJH + 128]
                ll = sb["A_wp"][:, b0 + SPJH + 128:b0 + SPJH + 256]
                th = sb["A_tth"][:, chunk:2 * chunk]
                tl = sb["A_ttl"][:, chunk:2 * chunk]
                e.matmul(ps[bank][:, 0:chunk], lh, th,
                         start=False, stop=False)
                e.matmul(ps[bank][:, 0:chunk], lh, tl,
                         start=False, stop=False)
                last = e.matmul(ps[bank][:, 0:chunk], ll, th,
                                start=False, stop=True)
                return last
            pe(pjB, [aregB,
                     Bf("A_tt", 0, 128, 0, 2 * chunk)]
               + h1hreg[KF2:] + h1lreg[KF2:],
               [BP(bank)], f"pB_{tag}{d}")
            out_evac(d, bank)
            if stats_x is not None and d >= 1:
                stats_d(d - 1)
        if stats_x is not None:
            stats_d(KD - 1)

    def prep_raw(dst_cols, src_ap, dram_buf, wb, tag):
        dma("sp", sb[wb][:, 0:dst_cols], src_ap,
            [dram_buf], [Bf(wb, 0, 128, 0, dst_cols)], f"pr_{tag}")

    def prep_maxabs(wcols, mst_col, wb, tag):
        dve(lambda e: e.reduce_max(
            sb["A_c"][:, C_MST + mst_col:C_MST + mst_col + 1],
            sb[wb][:, 0:wcols], axis=mybir.AxisListType.X,
            apply_absolute_value=True),
            [Bf(wb, 0, 128, 0, wcols)],
            [BC(C_MST + mst_col, 1)], f"mx_{tag}")

    for lp in range(FCL):
        wb = "A_wa" if lp % 2 == 0 else "A_wb"
        prep_raw(D, wfc_l[lp][:, 0:D], BD("wfcl", lp, lp + 1), wb, f"mf{lp}")
        prep_maxabs(D, lp, wb, f"mf{lp}")
    dve(lambda e: e.reduce_max(
        sb["A_c"][:, C_MST + 8:C_MST + 9],
        sb["A_c"][:, C_MST:C_MST + FCL], axis=mybir.AxisListType.X),
        [BC(C_MST, FCL)], [BC(C_MST + 8, 1)], "mxcf")
    for lp in range(PJL):
        prep_raw(SPJH, wpjA_l[lp], BD("wpjal", lp, lp + 1), "A_wa",
                 f"ma{lp}")
        prep_maxabs(SPJH, 10 + 2 * lp, "A_wa", f"ma{lp}")
        prep_raw(SPJH, wpjB_l[lp][:, 0:SPJH], BD("wpjbl", lp, lp + 1),
                 "A_wb", f"mb{lp}")
        prep_maxabs(SPJH, 11 + 2 * lp, "A_wb", f"mb{lp}")
    dve(lambda e: e.reduce_max(
        sb["A_c"][:, C_MST + 9:C_MST + 10],
        sb["A_c"][:, C_MST + 10:C_MST + 10 + 2 * PJL],
        axis=mybir.AxisListType.X),
        [BC(C_MST + 10, 2 * PJL)], [BC(C_MST + 9, 1)], "mxcp")
    dma("pool", mxin[:, :], sb["A_c"][:, C_MST + 8:C_MST + 10],
        [BC(C_MST + 8, 2)], [BD("mxin", 0, 1)], "mxs")

    def arfn(e):
        return e.collective_compute(
            "AllReduce", ALU.max, replica_groups=[list(range(NCORES))],
            ins=[mxin.opt()], outs=[mxout.opt()])
    p.op("pool", arfn, reads=[BD("mxin", 0, 1)], writes=[BD("mxout", 0, 1)],
         tag="ar")

    def load_x(ck):
        for d in range(KD):
            dma("act", sb["A_x"][:, d * chunk:(d + 1) * chunk],
                x_in[d * 128:(d + 1) * 128, ck * chunk:(ck + 1) * chunk],
                [BD("x_in", 0, 1)], [BT("A_x", d)], f"lx{ck}_{d}")
    load_x(0)
    ln_stats("A_x", False, "A0n1")
    ln_apply_pair("A_x", "A0n1")

    dma("pool", sb["A_c"][:, C_MST + 32:C_MST + 34], mxout[:, :],
        [BD("mxout", 0, 1)], [BC(C_MST + 32, 2)], "mxl")
    finish_scale(32, 0, 4, "fc")
    finish_scale(33, 2, 5, "pj")
    dve(lambda e: e.tensor_scalar_mul(
        sb["A_c"][:, C_SCAL + 6:C_SCAL + 7],
        sb["A_c"][:, C_SCAL + 3:C_SCAL + 4], 0.5),
        [BC(C_SCAL + 3, 1)], [BC(C_SCAL + 6, 1)], "shalf")

    for lp in range(FCL):
        wb = "A_wa" if lp % 2 == 0 else "A_wb"
        wbh = wb + "h"
        prep_raw(SFC, wfc_l[lp], BD("wfcl", lp, lp + 1), wb, f"qf{lp}")
        quant_to_bf(wb, wbh, D, 0, f"pf{lp}")
        lora_split(wb, wbh, D, D, f"pf{lp}")
        dma("pool", prep_fc[:, lp * QFC:(lp + 1) * QFC], sb[wbh][:, 0:QFC],
            [BA(wbh, 0, 128, 0, QFC)], [BD("pfc", lp, lp + 1)], f"sf{lp}")

    def ag1fn(e):
        return e.collective_compute(
            "AllGather", ALU.bypass, replica_groups=[list(range(NCORES))],
            ins=[prep_fc.opt()], outs=[prep_fc_all.opt()])
    p.op("pool", ag1fn, reads=[BD("pfc", 0, FCL)],
         writes=[BD("agfc", 0, 1)], tag="ag1")

    LBIG = 4 * KI * chunk >= 6 * 4096
    P_WB, P_WH = ("A_lam", "A_lmh") if LBIG else ("A_wa", "A_wah")
    prep2 = []
    for lp in range(PJL):
        def pa(lp=lp):
            dma("act", sb[P_WB][:, 0:SPJH], wpjA_l[lp],
                [BD("wpjal", lp, lp + 1)],
                [Bf(P_WB, 0, 128, 0, SPJH)], f"qa{lp}")
            quant_to_bf(P_WB, P_WH, SPJH, 2, f"pa{lp}")
            dma("pool", prep_pj[:, RO_PJA + lp * SPJH:
                                RO_PJA + (lp + 1) * SPJH],
                sb[P_WH][:, 0:SPJH],
                [BA(P_WH, 0, 128, 0, SPJH)], [BD("ppj", 0, 1)],
                f"sa{lp}")
        prep2.append(pa)

        def pb(lp=lp):
            dma("act", sb[P_WB][:, 0:SPJ - SPJH], wpjB_l[lp],
                [BD("wpjbl", lp, lp + 1)],
                [Bf(P_WB, 0, 128, 0, SPJ - SPJH)], f"qb{lp}")
            quant_to_bf(P_WB, P_WH, SPJH, 2, f"pb{lp}")
            lora_split(P_WB, P_WH, SPJH, SPJH, f"pb{lp}")
            dma("pool", prep_pj[:, RO_PJB + lp * QPJB:
                                RO_PJB + (lp + 1) * QPJB],
                sb[P_WH][:, 0:QPJB],
                [BA(P_WH, 0, 128, 0, QPJB)], [BD("ppj", 0, 1)],
                f"sb{lp}")
        prep2.append(pb)

    def agpj():
        def agfn(e):
            return e.collective_compute(
                "AllGather", ALU.bypass,
                replica_groups=[list(range(NCORES))],
                ins=[prep_pj.opt()], outs=[prep_pj_all.opt()])
        p.op("pool", agfn, reads=[BD("ppj", 0, 1)],
             writes=[BD("agpj", 0, 1)], tag="agpj")
    prep2.append(agpj)

    def prep_split(src_strip, dram_buf, wcols, rest_off, tag,
                   pmx=None, pdep="pmx"):
        PC = min(1024, wcols)
        if LBIG:
            rwb, rwh = "A_lam", "A_lmh"
            RB = KI * chunk // 2
        else:
            rwb, rwh = "A_wb", "A_wbh"
            RB = 0
        HB = 2 * RB + 2 * PC if LBIG else 2048
        for pcn in range(wcols // PC):
            o = pcn * PC
            raw = sb[rwb][:, RB:RB + PC]
            rreg = Bf(rwb, 0, 128, RB, RB + PC)
            dma("act", raw, src_strip[:, o:o + PC], [dram_buf], [rreg],
                f"sr_{tag}{pcn}")
            hi = sb[rwh][:, HB:HB + PC]
            hireg = BA(rwh, 0, 128, HB, HB + PC)
            lo = sb[rwh][:, HB + PC:HB + 2 * PC]
            loreg = BA(rwh, 0, 128, HB + PC, HB + 2 * PC)
            act(lambda e, raw=raw, hi=hi: e.activation(
                out=hi, in_=raw, func=AF.Copy),
                [rreg], [hireg], f"sh_{tag}{pcn}")
            dve(lambda e, raw=raw, hi=hi, lo=lo: e.tensor_sub(lo, raw, hi),
                [rreg, hireg], [loreg], f"sl_{tag}{pcn}")
            dma("pool", pmx[:, rest_off + o:rest_off + o + PC], hi,
                [hireg], [BD(pdep, 0, 1)], f"s1_{tag}{pcn}")
            dma("pool",
                pmx[:, rest_off + wcols + o:rest_off + wcols + o + PC],
                lo, [loreg], [BD(pdep, 0, 1)], f"s2_{tag}{pcn}")

    for lp in range(WDTL):
        prep2.append(lambda lp=lp: prep_split(
            wdt_l[lp], BD("wdtl", lp, lp + 1), INNER,
            RO_WDT + lp * 2 * INNER, f"dt{lp}", prep_mxB, "pmxB"))
    for lp in range(WOUTL):
        prep2.append(lambda lp=lp: prep_split(
            wout_l[lp], BD("woutl", lp, lp + 1), INNER,
            RO_WOUT + lp * 2 * INNER, f"wo{lp}", prep_mxB, "pmxB"))

    def emit_ag2(which):
        def ag2fn(e):
            return e.collective_compute(
                "AllGather", ALU.bypass,
                replica_groups=[list(range(NCORES))],
                ins=[prep_mxB.opt()], outs=[prep_mxB_all.opt()])
        p.op("pool", ag2fn, reads=[BD("pmxB", 0, 1)],
             writes=[BD("agmxB", 0, 1)], tag="ag2B")

    for ck in range(nch):

        def evac1(d, bank, ck=ck):
            tsl = ST_T2 if d % 2 == 0 else ST_T1
            dve(lambda e, bank=bank, d=d, tsl=tsl: e.tensor_scalar(
                out=st(tsl), in0=ps[bank][:, 0:chunk],
                scalar1=sb["A_c"][:, C_SCAL + 6:C_SCAL + 7],
                scalar2=sb["A_c"][:, C_BPH + d:C_BPH + d + 1],
                op0=ALU.mult, op1=ALU.add),
                [BP(bank), BC(C_SCAL + 6, 1), BC(C_BPH + d, 1)],
                [streg(tsl)], f"e1_{ck}{d}")
            dve(lambda e, d=d, tsl=tsl: e.tensor_add(
                sb["A_x"][:, d * chunk:(d + 1) * chunk],
                sb["A_x"][:, d * chunk:(d + 1) * chunk], st(tsl)),
                [BT("A_x", d), streg(tsl)], [BT("A_x", d)],
                f"x1_{ck}{d}")
        ffn(evac1, f"A{ck}", bg=(prep2 if ck == 0 else None),
            after_fc=((lambda: emit_ag2("B")) if ck == 0 else None),
            stats_x="A_x")
        dma("pool", x1_spill[ck], sb["A_x"][:, 0:KD * chunk],
            [Bf("A_x", 0, 128, 0, KD * chunk)], [BD("x1sp", ck, ck + 1)],
            f"sx1{ck}")

        ln_stats_chain(True, f"A{ck}n2")
        for d in range(KD):
            xs = sb["A_x"][:, d * chunk:(d + 1) * chunk]
            os_ = sb["A_xin"][:, d * chunk:(d + 1) * chunk]
            dve(lambda e, xs=xs, os_=os_: e.tensor_sub(os_, xs, st(ST_A)),
                [BT("A_x", d), streg(ST_A)], [BT("A_xin", d)],
                f"la_A{ck}n2{d}")
            dve(lambda e, os_=os_: e.tensor_mul(os_, os_, st(ST_RSTD)),
                [BT("A_xin", d), streg(ST_RSTD)], [BT("A_xin", d)],
                f"lb_A{ck}n2{d}")
            t1 = sb["A_scr"][:, (d % 2) * chunk:(d % 2 + 1) * chunk]
            t1reg = BT("A_scr", d % 2)
            t2 = sb["A_scr"][:, (2 + d % 2) * chunk:(3 + d % 2) * chunk]
            t2reg = BT("A_scr", 2 + d % 2)
            hi = sb["A_xnh"][:, d * chunk:(d + 1) * chunk]
            hireg = BAT("A_xnh", d)
            lo = sb["A_xnl"][:, d * chunk:(d + 1) * chunk]
            loreg = BAT("A_xnl", d)
            wn_ap = sb["A_c"][:, C_WNORM + d:C_WNORM + d + 1]
            dve(lambda e, d=d, t1=t1: e.tensor_mul(
                t1, sb["A_xin"][:, d * chunk:(d + 1) * chunk], st(ST_RMS)),
                [BT("A_xin", d), streg(ST_RMS)], [t1reg], f"hm_{ck}{d}")
            act(lambda e, t1=t1, hi=hi, wn_ap=wn_ap: e.activation(
                out=hi, in_=t1, func=AF.Copy, scale=wn_ap),
                [t1reg, BC(C_WNORM + d, 1)], [hireg], f"hh_{ck}{d}")
            act(lambda e, t1=t1, t2=t2, wn_ap=wn_ap: e.activation(
                out=t2, in_=t1, func=AF.Copy, scale=wn_ap),
                [t1reg, BC(C_WNORM + d, 1)], [t2reg], f"hf_{ck}{d}")
            dve(lambda e, t2=t2, hi=hi, lo=lo: e.tensor_sub(lo, t2, hi),
                [t2reg, hireg], [loreg], f"hl_{ck}{d}")
        dma("pool", xin_spill[ck], sb["A_xin"][:, 0:KD * chunk],
            [Bf("A_xin", 0, 128, 0, KD * chunk)], [BD("xnsp", ck, ck + 1)],
            f"sxn{ck}")

        xnh = [sb["A_xnh"][:, k * chunk:(k + 1) * chunk] for k in range(KD)]
        xnl = [sb["A_xnl"][:, k * chunk:(k + 1) * chunk] for k in range(KD)]
        def win_split(m):
            wb = "A_wa" if m % 2 == 0 else "A_wb"
            wbh = wb + "h"
            raw = sb[wb][:, D:2 * D]
            rawreg = Bf(wb, 0, 128, D, 2 * D)
            dma("sp", raw, win_f[m], [BD("winf", 0, 1)], [rawreg],
                f"li0{m}")
            hi = sb[wbh][:, 0:D]
            hireg = BA(wbh, 0, 128, 0, D)
            lo = sb[wbh][:, D:2 * D]
            loreg = BA(wbh, 0, 128, D, 2 * D)
            act(lambda e, raw=raw, hi=hi: e.activation(
                out=hi, in_=raw, func=AF.Copy),
                [rawreg], [hireg], f"lih0{m}")
            dve(lambda e, raw=raw, hi=hi, lo=lo: e.tensor_sub(
                lo, raw, hi),
                [rawreg, hireg], [loreg], f"lil0{m}")
            dma("pool", win_cache[:, m * 2 * D:(m + 1) * 2 * D],
                sb[wbh][:, 0:2 * D], [BA(wbh, 0, 128, 0, 2 * D)],
                [BD("wic", m, m + 1)], f"lis0{m}")

        for m in range(2 * KI):
            wb = "A_wa" if m % 2 == 0 else "A_wb"
            wbh = wb + "h"
            wqreg = BA(wbh, 0, 128, 0, 2 * D)
            if ck == 0:
                if m == 0:
                    win_split(0)
                if m + 1 < 2 * KI:
                    win_split(m + 1)
            else:
                dma("sp", sb[wbh][:, 0:2 * D],
                    win_cache[:, m * 2 * D:(m + 1) * 2 * D],
                    [BD("wic", m, m + 1)],
                    [wqreg], f"li{ck}{m}")
            bank = PS_MAIN[m % len(PS_MAIN)]

            def wik(e, wbh=wbh, bank=bank, k=0, first=False, last_=False):
                wh = sb[wbh][:, k * 128:(k + 1) * 128]
                wl = sb[wbh][:, D + k * 128:D + (k + 1) * 128]
                e.matmul(ps[bank][:, 0:chunk], wh, xnh[k],
                         start=first, stop=False)
                e.matmul(ps[bank][:, 0:chunk], wh, xnl[k],
                         start=False, stop=False)
                return e.matmul(ps[bank][:, 0:chunk], wl, xnh[k],
                                start=False, stop=last_)
            if m == 0:
                for k in range(KD):
                    def wik1(e, k=k, wbh=wbh, bank=bank):
                        return wik(e, wbh, bank, k, k == 0, k == KD - 1)
                    pe(wik1, [wqreg, BAT("A_xnh", k), BAT("A_xnl", k)],
                       [BP(bank)], f"wi_{ck}{m}_{k}")
            else:
                def wifn(e, wbh=wbh, bank=bank):
                    last = None
                    for k in range(KD):
                        last = wik(e, wbh, bank, k, k == 0, k == KD - 1)
                    return last
                pe(wifn, [wqreg] + [BAT("A_xnh", k) for k in range(KD)]
                   + [BAT("A_xnl", k) for k in range(KD)],
                   [BP(bank)], f"wi_{ck}{m}")
            if m < KI:
                tsl = ST_T1 if m % 2 == 0 else ST_T2
                act(lambda e, bank=bank, tsl=tsl: e.activation(
                    out=st(tsl), in_=ps[bank][:, 0:chunk], func=AF.Silu),
                    [BP(bank)], [streg(tsl)], f"we_{ck}{m}")
                act(lambda e, m=m, tsl=tsl: e.activation(
                    out=sb["A_bgh"][:, m * chunk:(m + 1) * chunk],
                    in_=st(tsl), func=AF.Copy),
                    [streg(tsl)], [BAT("A_bgh", m)], f"wh_{ck}{m}")
                dve(lambda e, m=m, tsl=tsl: e.tensor_sub(
                    sb["A_bgh"][:, (KI + m) * chunk:(KI + m + 1) * chunk],
                    st(tsl),
                    sb["A_bgh"][:, m * chunk:(m + 1) * chunk]),
                    [streg(tsl), BAT("A_bgh", m)], [BAT("A_bgh", KI + m)],
                    f"wl_{ck}{m}")
            else:
                j = m - KI
                act(lambda e, bank=bank, j=j: e.activation(
                    out=sb["A_big"][:, (KI + j) * chunk:(KI + j + 1) * chunk],
                    in_=ps[bank][:, 0:chunk], func=AF.Silu),
                    [BP(bank)], [BT("A_big", KI + j)], f"we_{ck}{m}")

        if ck + 1 < nch:
            load_x(ck + 1)
            ln_stats("A_x", False, f"A{ck + 1}n1")
            ln_apply_pair("A_x", f"A{ck + 1}n1")
        for j in range(KI):
            wb = "A_wa" if j % 2 == 0 else "A_wb"
            wbh = wb + "h"
            dma("sp", sb[wbh][:, 0:2 * INNER],
                prep_mxB_all[j % NCORES][
                    :, RO_WDT + (j // NCORES) * 2 * INNER:
                    RO_WDT + (j // NCORES + 1) * 2 * INNER],
                [BD("agmxB", 0, 1)],
                [BA(wbh, 0, 128, 0, 2 * INNER)], f"ld{ck}{j}")
            wqreg = BA(wbh, 0, 128, 0, 2 * INNER)
            bank = PS_MAIN[j % len(PS_MAIN)]

            def dtfn(e, wbh=wbh, bank=bank):
                last = None
                for k in range(KI):
                    wh = sb[wbh][:, k * 128:(k + 1) * 128]
                    wl = sb[wbh][:, INNER + k * 128:INNER + (k + 1) * 128]
                    uh = sb["A_bgh"][:, k * chunk:(k + 1) * chunk]
                    ul = sb["A_bgh"][:, (KI + k) * chunk:(KI + k + 1) * chunk]
                    e.matmul(ps[bank][:, 0:chunk], wh, uh,
                             start=(k == 0), stop=False)
                    e.matmul(ps[bank][:, 0:chunk], wh, ul,
                             start=False, stop=False)
                    last = e.matmul(ps[bank][:, 0:chunk], wl, uh,
                                    start=False, stop=(k == KI - 1))
                return last
            pe(dtfn, [wqreg] + [BAT("A_bgh", k) for k in range(2 * KI)],
               [BP(bank)], f"dt_{ck}{j}")
            act(lambda e, bank=bank, j=j: e.activation(
                out=sb["A_lam"][:, j * chunk:(j + 1) * chunk],
                in_=ps[bank][:, 0:chunk], func=AF.Sigmoid,
                bias=sb["A_c"][:, C_BDT + j:C_BDT + j + 1]),
                [BP(bank), BC(C_BDT + j, 1)], [BT("A_lam", j)],
                f"de_{ck}{j}")

            lam_j = sb["A_lam"][:, j * chunk:(j + 1) * chunk]
            lamreg = BT("A_lam", j)
            u_j = sb["A_scr"][:, 3 * chunk:4 * chunk]
            ureg = BT("A_scr", 3)
            sz_j = sb["A_big"][:, (KI + j) * chunk:(KI + j + 1) * chunk]
            szreg = BT("A_big", KI + j)
            dve(lambda e, u_j=u_j, j=j: e.tensor_add(
                u_j, sb["A_bgh"][:, j * chunk:(j + 1) * chunk],
                sb["A_bgh"][:, (KI + j) * chunk:(KI + j + 1) * chunk]),
                [BAT("A_bgh", j), BAT("A_bgh", KI + j)], [ureg],
                f"ur_{ck}{j}")
            b_j = sb["A_scr"][:, 0:chunk]
            breg = BT("A_scr", 0)
            s_j = sb["A_scr"][:, chunk:2 * chunk]
            sreg = BT("A_scr", 1)
            p_j = sb["A_scr"][:, 2 * chunk:3 * chunk]
            preg = BT("A_scr", 2)
            slast = sb["A_c"][:, C_SLAST + j:C_SLAST + j + 1]
            slreg = BC(C_SLAST + j, 1)
            rlast = sb["A_c"][:, C_RLAST + j:C_RLAST + j + 1]
            rlreg = BC(C_RLAST + j, 1)
            dve(lambda e, lam_j=lam_j, u_j=u_j, b_j=b_j: e.tensor_mul(
                b_j, lam_j, u_j), [lamreg, ureg], [breg], f"b1_{ck}{j}")
            dve(lambda e, u_j=u_j, b_j=b_j: e.tensor_sub(b_j, u_j, b_j),
                [ureg, breg], [breg], f"b2_{ck}{j}")
            dve(lambda e, lam_j=lam_j, b_j=b_j, s_j=s_j, slast=slast:
                e.tensor_tensor_scan(out=s_j, data0=lam_j, data1=b_j,
                                     initial=slast, op0=ALU.mult,
                                     op1=ALU.add),
                [lamreg, breg, slreg], [sreg], f"sc_{ck}{j}")
            dve(lambda e, lam_j=lam_j, p_j=p_j, rlast=rlast:
                e.tensor_tensor_scan(
                    out=p_j, data0=lam_j,
                    data1=sb["A_c"][:, C_ONESC:C_ONESC + chunk],
                    initial=rlast, op0=ALU.mult, op1=ALU.mult),
                [lamreg, BC(C_ONESC, chunk), rlreg], [preg], f"cp_{ck}{j}")
            dve(lambda e, s_j=s_j, slast=slast: e.tensor_copy(
                slast, s_j[:, chunk - 1:chunk]),
                [sreg], [slreg], f"sl_{ck}{j}")
            dve(lambda e, p_j=p_j, rlast=rlast: e.tensor_copy(
                rlast, p_j[:, chunk - 1:chunk]),
                [preg], [rlreg], f"rl_{ck}{j}")
            dve(lambda e, s_j=s_j, sz_j=sz_j: e.tensor_mul(s_j, s_j, sz_j),
                [sreg, szreg], [sreg], f"s3_{ck}{j}")
            dve(lambda e, p_j=p_j, sz_j=sz_j: e.tensor_mul(p_j, p_j, sz_j),
                [preg, szreg], [preg], f"g3_{ck}{j}")
            dma("pool", sv_spill[ck][:, j * chunk:(j + 1) * chunk], s_j,
                [sreg], [BD("svsp", ck * KI + j, ck * KI + j + 1)],
                f"ss_{ck}{j}")
            dma("pool", g_spill[ck][:, j * chunk:(j + 1) * chunk], p_j,
                [preg], [BD("gsp", ck * KI + j, ck * KI + j + 1)],
                f"sg_{ck}{j}")


    def g_ap(j):
        if j < KD:
            return (sb["A_xin"][:, j * chunk:(j + 1) * chunk],
                    BT("A_xin", j))
        return (sb["A_xn"][:, (j - KD) * chunk:(j - KD + 1) * chunk],
                BT("A_xn", j - KD))

    def prologue_b_loads(ck, with_x1=False):
        dma("act", sb["A_lam"][:, 0:KI * chunk], sv_spill[ck],
            [BD("svsp", ck * KI, (ck + 1) * KI)],
            [Bf("A_lam", 0, 128, 0, KI * chunk)], f"rv{ck}")
        gh = KD * chunk
        dma("act", sb["A_xin"][:, 0:gh], g_spill[ck][:, 0:gh],
            [BD("gsp", ck * KI, (ck + 1) * KI)],
            [Bf("A_xin", 0, 128, 0, gh)], f"rg{ck}a")
        dma("sp", sb["A_xn"][:, 0:gh], g_spill[ck][:, gh:2 * gh],
            [BD("gsp", ck * KI, (ck + 1) * KI)],
            [Bf("A_xn", 0, 128, 0, gh)], f"rg{ck}b")
        if with_x1:
            dma("act", sb["A_x"][:, 0:KD * chunk], x1_spill[ck],
                [BD("x1sp", ck, ck + 1)],
                [Bf("A_x", 0, 128, 0, KD * chunk)], f"r1{ck}")

    def prologue_b_corr(ck):
        for j in range(KI):
            gA, gR = g_ap(j)
            svf = sb["A_scr"][:, (j % 2) * chunk:(j % 2 + 1) * chunk]
            svfreg = BT("A_scr", j % 2)
            svh = sb["A_lmh"][:, 2 * j * chunk:2 * j * chunk + chunk]
            svhreg = BA("A_lmh", 0, 128, 2 * j * chunk,
                        2 * j * chunk + chunk)
            svl = sb["A_lmh"][:, 2 * j * chunk + chunk:2 * (j + 1) * chunk]
            svlreg = BA("A_lmh", 0, 128, 2 * j * chunk + chunk,
                        2 * (j + 1) * chunk)
            dve(lambda e, j=j, svf=svf, gA=gA: e.scalar_tensor_tensor(
                out=svf, in0=gA,
                scalar=sb["A_c"][:, C_CARRY + j:C_CARRY + j + 1],
                in1=sb["A_lam"][:, j * chunk:(j + 1) * chunk],
                op0=ALU.mult, op1=ALU.add),
                [gR, BC(C_CARRY + j, 1), BT("A_lam", j)],
                [svfreg], f"sv{ck}_{j}")
            act(lambda e, svf=svf, svh=svh: e.activation(
                out=svh, in_=svf, func=AF.Copy),
                [svfreg], [svhreg], f"svh{ck}_{j}")
            pool(lambda e, svf=svf, svh=svh, svl=svl: e.tensor_sub(
                svl, svf, svh),
                 [svfreg, svhreg], [svlreg], f"svl{ck}_{j}")
        dma("act", sb["A_xin"][:, 0:KD * chunk], xin_spill[ck],
            [BD("xnsp", ck, ck + 1)],
            [Bf("A_xin", 0, 128, 0, KD * chunk)], f"rn{ck}")

    def prologue_b(ck):
        prologue_b_loads(ck)
        prologue_b_corr(ck)

    prologue_b_loads(0, with_x1=True)
    dma("pool", agin[0], sb["A_c"][:, C_RLAST:C_RLAST + KI],
        [BC(C_RLAST, KI)], [BD("agin", 0, 1)], "agA")
    dma("pool", agin[1], sb["A_c"][:, C_SLAST:C_SLAST + KI],
        [BC(C_SLAST, KI)], [BD("agin", 1, 2)], "agB")

    def ccfn(e):
        return e.collective_compute(
            "AllGather", ALU.bypass, replica_groups=[list(range(NCORES))],
            ins=[agin.opt()], outs=[agout.opt()])
    p.op("pool", ccfn, reads=[BD("agin", 0, 2)], writes=[BD("agout", 0, 16)],
         tag="cc")
    for r in range(NCORES):
        dma("pool", sb["A_c"][:, C_AG + r * KI:C_AG + (r + 1) * KI],
            agout[r, 0], [BD("agout", 0, 16)], [BC(C_AG + r * KI, KI)],
            f"ga{r}")
        dma("pool", sb["A_c"][:, C_BG + r * KI:C_BG + (r + 1) * KI],
            agout[r, 1], [BD("agout", 0, 16)], [BC(C_BG + r * KI, KI)],
            f"gb{r}")

    def AGc(r):
        return sb["A_c"][:, C_AG + r * KI:C_AG + (r + 1) * KI]

    def BGc(r):
        return sb["A_c"][:, C_BG + r * KI:C_BG + (r + 1) * KI]

    carry = sb["A_c"][:, C_CARRY:C_CARRY + KI]
    carryreg = BC(C_CARRY, KI)
    dve(lambda e: e.memset(carry, 0.0), [], [carryreg], "c0")
    agrd = [BC(C_AG, NCORES * KI), BC(C_BG, NCORES * KI)]
    for b in range(2):
        base = 4 * b
        c1 = sb["A_c"][:, C_CAR:C_CAR + KI]
        c2 = sb["A_c"][:, C_CAR + KI:C_CAR + 2 * KI]
        c3 = sb["A_c"][:, C_CAR + 2 * KI:C_CAR + 3 * KI]
        r1 = BC(C_CAR, KI)
        r2 = BC(C_CAR + KI, KI)
        r3 = BC(C_CAR + 2 * KI, KI)
        dve(lambda e, c1=c1, base=base: e.tensor_copy(c1, BGc(base)),
            agrd, [r1], f"c1_{b}")
        dve(lambda e, c2=c2, c1=c1, base=base: e.tensor_mul(
            c2, AGc(base + 1), c1), agrd + [r1], [r2], f"c2a_{b}")
        dve(lambda e, c2=c2, base=base: e.tensor_add(
            c2, BGc(base + 1), c2), agrd + [r2], [r2], f"c2b_{b}")
        dve(lambda e, c3=c3, c2=c2, base=base: e.tensor_mul(
            c3, AGc(base + 2), c2), agrd + [r2], [r3], f"c3a_{b}")
        dve(lambda e, c3=c3, base=base: e.tensor_add(
            c3, BGc(base + 2), c3), agrd + [r3], [r3], f"c3b_{b}")
        for i, (cx, rx) in enumerate(((c1, r1), (c2, r2), (c3, r3))):
            col = C_CMASK + base + 1 + i
            dve(lambda e, cx=cx, col=col: e.scalar_tensor_tensor(
                out=carry, in0=cx, scalar=sb["A_c"][:, col:col + 1],
                in1=carry, op0=ALU.mult, op1=ALU.add),
                [rx, BC(col, 1), carryreg], [carryreg], f"se_{b}{i}")

    for ck in range(nch):
        if ck == 0:
            prologue_b_corr(0)
        else:
            dma("act", sb["A_x"][:, 0:KD * chunk], x1_spill[ck],
                [BD("x1sp", ck, ck + 1)],
                [Bf("A_x", 0, 128, 0, KD * chunk)], f"r1{ck}")
        for d in range(KD):
            wb = "A_wa" if d % 2 == 0 else "A_wb"
            wbh = wb + "h"
            dma("sp", sb[wbh][:, 0:2 * INNER],
                prep_mxB_all[d % NCORES][
                    :, RO_WOUT + (d // NCORES) * 2 * INNER:
                    RO_WOUT + (d // NCORES + 1) * 2 * INNER],
                [BD("agmxB", 0, 1)],
                [BA(wbh, 0, 128, 0, 2 * INNER)], f"lo{ck}{d}")
            wqreg = BA(wbh, 0, 128, 0, 2 * INNER)
            bank = PS_MAIN[d % len(PS_MAIN)]

            def wok(e, wbh=wbh, bank=bank, k=0):
                wh = sb[wbh][:, k * 128:(k + 1) * 128]
                wl = sb[wbh][:, INNER + k * 128:INNER + (k + 1) * 128]
                sh = sb["A_lmh"][:, 2 * k * chunk:2 * k * chunk + chunk]
                sl = sb["A_lmh"][:, 2 * k * chunk + chunk:
                                 2 * (k + 1) * chunk]
                e.matmul(ps[bank][:, 0:chunk], wh, sh,
                         start=(k == 0), stop=False)
                e.matmul(ps[bank][:, 0:chunk], wh, sl,
                         start=False, stop=False)
                return e.matmul(ps[bank][:, 0:chunk], wl, sh,
                                start=False, stop=(k == KI - 1))
            if d == 0:
                for k in range(KI):
                    def wok1(e, wbh=wbh, bank=bank, k=k):
                        return wok(e, wbh, bank, k)
                    pe(wok1, [wqreg,
                              BA("A_lmh", 0, 128, 2 * k * chunk,
                                 2 * (k + 1) * chunk)],
                       [BP(bank)], f"wo_{ck}{d}_{k}")
            else:
                def wofn(e, wbh=wbh, bank=bank):
                    last = None
                    for k in range(KI):
                        last = wok(e, wbh, bank, k)
                    return last
                pe(wofn, [wqreg, BA("A_lmh", 0, 128, 0, 2 * KI * chunk)],
                   [BP(bank)], f"wo_{ck}{d}")
            tsl = ST_T1 if d % 2 == 0 else ST_T2
            dve(lambda e, bank=bank, d=d, tsl=tsl: e.tensor_add(
                st(tsl), ps[bank][:, 0:chunk],
                sb["A_xin"][:, d * chunk:(d + 1) * chunk]),
                [BP(bank), BT("A_xin", d)], [streg(tsl)], f"y1{ck}{d}")
            dve(lambda e, d=d, tsl=tsl: e.tensor_add(
                sb["A_x"][:, d * chunk:(d + 1) * chunk], st(tsl),
                sb["A_x"][:, d * chunk:(d + 1) * chunk]),
                [streg(tsl), BT("A_x", d)], [BT("A_x", d)],
                f"y2{ck}{d}")
        ln_stats("A_x", False, f"B{ck}n3")
        ln_apply_pair("A_x", f"B{ck}n3")

        def evac2(d, bank, ck=ck):
            tsl = ST_T2 if d % 2 == 0 else ST_T1
            dve(lambda e, bank=bank, d=d, tsl=tsl: e.tensor_scalar(
                out=st(tsl), in0=ps[bank][:, 0:chunk],
                scalar1=sb["A_c"][:, C_SCAL + 6:C_SCAL + 7],
                scalar2=sb["A_c"][:, C_BPH + d:C_BPH + d + 1],
                op0=ALU.mult, op1=ALU.add),
                [BP(bank), BC(C_SCAL + 6, 1), BC(C_BPH + d, 1)],
                [streg(tsl)], f"e2_{ck}{d}")
            dve(lambda e, d=d, tsl=tsl: e.tensor_add(
                sb["A_x"][:, d * chunk:(d + 1) * chunk],
                sb["A_x"][:, d * chunk:(d + 1) * chunk], st(tsl)),
                [BT("A_x", d), streg(tsl)], [BT("A_x", d)],
                f"o1_{ck}{d}")
            dma("pool",
                out_fm[d * 128:(d + 1) * 128, ck * chunk:(ck + 1) * chunk],
                sb["A_x"][:, d * chunk:(d + 1) * chunk],
                [BT("A_x", d)], [BD("out", ck * KD + d, ck * KD + d + 1)],
                f"so_{ck}{d}")
        ffn(evac2, f"B{ck}",
            after_fc=((lambda ck=ck: prologue_b(ck + 1))
                      if ck + 1 < nch else None))

    p.wait_all_end()

    with contextlib.ExitStack() as ctx:
        sems = {}
        for e in ENGINES:
            sems[("t", e)] = ctx.enter_context(nc.semaphore(f"t_{e}"))
        for e in DMA_ENGS:
            for k in range(QPOOL[e]):
                sems[("q", e, k)] = ctx.enter_context(
                    nc.semaphore(f"q_{e}{k}"))
        fns = p.replay_fns(sems)
        with nc.Block() as block:
            block.tensor(fns["pe"])
            block.scalar(fns["act"])
            block.vector(fns["dve"])
            block.gpsimd(fns["pool"])
            block.sync(fns["sp"])
    return nc



def pack_lhsT(W):
    out_dim, in_dim = W.shape
    S, Kc = out_dim // 128, in_dim // 128
    A = W.reshape(S, 128, Kc, 128).transpose(0, 3, 2, 1)
    return np.ascontiguousarray(A.reshape(S, 128, in_dim), dtype=np.float32)


def pack_lhsT_lora(W, Bl):
    S = W.shape[0] // 128
    base = pack_lhsT(W)
    lora = np.zeros((S, 128, 128), dtype=np.float32)
    for s in range(S):
        lora[s, 0:R, :] = Bl[s * 128:(s + 1) * 128, :].T
    return np.ascontiguousarray(np.concatenate([base, lora], axis=2))


def shard_strips(strips, nloc, core, c0=None, c1=None):
    n = strips.shape[0]
    cols = strips.shape[2] if c0 is None else c1 - c0
    out = np.zeros((nloc, 128, cols), dtype=np.float32)
    for lp in range(nloc):
        idx = core + NCORES * lp
        if idx < n:
            out[lp] = strips[idx] if c0 is None else strips[idx, :, c0:c1]
    return np.ascontiguousarray(out)


def build_inputs(full, tok):
    x = np.asarray(full["L0_attn_x"], dtype=np.float32)
    wfc = pack_lhsT_lora(np.asarray(full["L0_ffn_W_fc"], np.float32),
                         np.asarray(full["L0_ffn_B_fc"], np.float32))
    wpj = pack_lhsT_lora(np.asarray(full["L0_ffn_W_proj"], np.float32),
                         np.asarray(full["L0_ffn_B_proj"], np.float32))
    win = pack_lhsT(np.asarray(full["L0_attn_W_in"], np.float32))
    wdt = pack_lhsT(np.asarray(full["L0_attn_W_dt"], np.float32))
    wout = pack_lhsT(np.asarray(full["L0_attn_W_out"], np.float32))
    FCL = (KF + NCORES - 1) // NCORES
    PJL = (KD + NCORES - 1) // NCORES
    WINL = (2 * KI + NCORES - 1) // NCORES
    WDTL = (KI + NCORES - 1) // NCORES
    WOUTL = (KD + NCORES - 1) // NCORES
    afc = np.ascontiguousarray(
        np.asarray(full["L0_ffn_A_fc"], np.float32)
        .reshape(R, KD, 128).transpose(2, 1, 0).reshape(128, KD * R))
    apj = np.ascontiguousarray(
        np.asarray(full["L0_ffn_A_proj"], np.float32)
        .reshape(R, KF, 128).transpose(2, 1, 0).reshape(128, KF * R))
    bias_fc = np.ascontiguousarray(
        np.asarray(full["L0_ffn_b_fc"], np.float32).reshape(KF, 128).T)
    bias_pj = np.ascontiguousarray(
        np.asarray(full["L0_ffn_b_proj"], np.float32).reshape(KD, 128).T)
    bias_dt = np.ascontiguousarray(
        np.asarray(full["L0_attn_b_dt"], np.float32).reshape(KI, 128).T)
    wnorm = np.ascontiguousarray(
        np.asarray(full["L0_attn_w_norm"], np.float32).reshape(KD, 128).T)
    ident = np.eye(128, dtype=np.float32)

    ins = []
    for c in range(NCORES):
        bc, pos = c // 4, c % 4
        t0 = pos * tok
        x_fm = np.ascontiguousarray(x[bc, t0:t0 + tok, :].T)
        cmask = np.zeros((128, NCORES), dtype=np.float32)
        cmask[:, c] = 1.0
        ins.append(dict(
            x_fm=x_fm,
            wfc_l=shard_strips(wfc, FCL, c),
            wpjA_l=shard_strips(wpj, PJL, c, 0, SPJH),
            wpjB_l=shard_strips(wpj, PJL, c, SPJH, SPJ),
            win_f=np.ascontiguousarray(win),
            wdt_l=shard_strips(wdt, WDTL, c),
            wout_l=shard_strips(wout, WOUTL, c),
            afc=afc, apj=apj,
            bias_fc=bias_fc, bias_pj=bias_pj, bias_dt=bias_dt,
            wnorm=wnorm, ident=ident, cmask=cmask))
    return ins


def kernel(**inputs):
    tok, chunk = 2048, 512
    ins = build_inputs(inputs, tok)
    nc = build_program(tok, chunk)
    res = run_bass_kernel_spmd(nc, ins, core_ids=list(range(NCORES)))
    out = np.empty((2, 8192, D), dtype=np.float32)
    for c in range(NCORES):
        bc, pos = c // 4, c % 4
        t0 = pos * tok
        out[bc, t0:t0 + tok, :] = res.results[c]["out_fm"].T
    return out


# version 36
# speedup vs baseline: 1.0139x; 1.0139x over previous
import contextlib

import numpy as np

import concourse.bass as bass
import concourse.mybir as mybir
from concourse.bass_utils import run_bass_kernel_spmd

F32 = mybir.dt.float32
I32 = mybir.dt.int32
BF16 = mybir.dt.bfloat16
AF = mybir.ActivationFunctionType
ALU = mybir.AluOpType

D = 1024
FF = 4096
INNER = 2048
R = 16
NCORES = 8
EPS = 1e-5
QMAX = 127.0

KD = D // 128
KF = FF // 128
KI = INNER // 128

SFC = D + 128
SPJ = FF + 128
KF2 = KF // 2
SPJH = KF2 * 128
QFC = D + 256
QPJB = SPJH + 256



ENGINES = ("pe", "act", "dve", "pool", "sp")
DMA_ENGS = ("sp", "act", "pool")


class Buf:
    __slots__ = ("arena", "p0", "p1", "c0", "c1")

    def __init__(self, arena, p0, p1, c0, c1):
        self.arena, self.p0, self.p1, self.c0, self.c1 = arena, p0, p1, c0, c1

    def key(self):
        return (self.p0, self.p1, self.c0, self.c1)

    def overlaps(self, o):
        return (self.p0 < o.p1 and o.p0 < self.p1
                and self.c0 < o.c1 and o.c0 < self.c1)


QPOOL = {"sp": 12, "act": 8, "pool": 8}


class Prog:
    def __init__(self):
        self.items = {e: [] for e in ENGINES}
        self.tick = {e: 0 for e in ENGINES}
        self.qcnt = {e: 0 for e in DMA_ENGS}
        self.quse = {(e, k): 0 for e in DMA_ENGS for k in range(QPOOL[e])}
        self.marks = {e: {} for e in ENGINES}
        self.recs = {}
        self.nwaits = 0

    def _arena_recs(self, arena):
        return self.recs.setdefault(arena, {})

    def _wants(self, eng, semkey, val, waits):
        if val <= 0:
            return
        if semkey == ("t", "pe") and eng == "pe":
            return
        wm = self.marks[eng].get(semkey, 0)
        if val <= wm:
            return
        if val > waits.get(semkey, 0):
            waits[semkey] = val

    def _dep_read(self, eng, buf, waits):
        for key, rec in self._arena_recs(buf.arena).items():
            if rec["w"] is None:
                continue
            if Buf(buf.arena, *key).overlaps(buf):
                self._wants(eng, rec["w"][0], rec["w"][1], waits)

    def _dep_write(self, eng, buf, waits):
        for key, rec in self._arena_recs(buf.arena).items():
            if not Buf(buf.arena, *key).overlaps(buf):
                continue
            if rec["w"] is not None:
                self._wants(eng, rec["w"][0], rec["w"][1], waits)
            for semkey, val in rec["r"].items():
                self._wants(eng, semkey, val, waits)

    def op(self, eng, fn, reads=(), writes=(), dma=False, tag=""):
        waits = {}
        for b in reads:
            self._dep_read(eng, b, waits)
        for b in writes:
            self._dep_write(eng, b, waits)
        for semkey, val in sorted(waits.items(), key=lambda kv: str(kv[0])):
            self.items[eng].append(("wait", semkey, val))
            self.nwaits += 1
            if val > self.marks[eng].get(semkey, 0):
                self.marks[eng][semkey] = val
        if dma:
            slot = self.qcnt[eng] % QPOOL[eng]
            self.qcnt[eng] += 1
            semkey = ("q", eng, slot)
            prev = 16 * self.quse[(eng, slot)]
            if prev > self.marks[eng].get(semkey, 0):
                self.items[eng].append(("wait", semkey, prev))
                self.nwaits += 1
                self.marks[eng][semkey] = prev
            self.quse[(eng, slot)] += 1
            val, amt = 16 * self.quse[(eng, slot)], 16
        else:
            if eng == "pool" and self.tick[eng] > 0:
                prev = self.tick[eng]
                if prev > self.marks[eng].get(("t", eng), 0):
                    self.items[eng].append(("wait", ("t", eng), prev))
                    self.nwaits += 1
                    self.marks[eng][("t", eng)] = prev
            self.tick[eng] += 1
            semkey, val, amt = ("t", eng), self.tick[eng], 1
        self.items[eng].append(("call", fn, semkey, amt, tag))
        done = (semkey, val)
        for b in reads:
            rec = self._arena_recs(b.arena).setdefault(
                b.key(), {"w": None, "r": {}})
            if val > rec["r"].get(semkey, 0):
                rec["r"][semkey] = val
        for b in writes:
            rec = self._arena_recs(b.arena).setdefault(
                b.key(), {"w": None, "r": {}})
            rec["w"] = done
            rec["r"] = {}
        return done

    def wait_all_end(self):
        for e in DMA_ENGS:
            for k in range(QPOOL[e]):
                v = 16 * self.quse[(e, k)]
                if v > self.marks[e].get(("q", e, k), 0):
                    self.items[e].append(("wait", ("q", e, k), v))

    def replay_fns(self, sems):
        def mk(eng):
            def body(engobj):
                for it in self.items[eng]:
                    if it[0] == "wait":
                        engobj.wait_ge(sems[it[1]], it[2])
                    else:
                        inst = it[1](engobj)
                        inst.then_inc(sems[it[2]], it[3])
            return body
        return {e: mk(e) for e in ENGINES}



def build_program(tok, chunk):
    nch = tok // chunk
    nc = bass.Bass("TRN2", target_bir_lowering=False)

    def din(name, shape):
        return nc.dram_tensor(name, shape, F32, kind="ExternalInput").ap()

    FCL = (KF + NCORES - 1) // NCORES
    PJL = (KD + NCORES - 1) // NCORES
    WINL = (2 * KI + NCORES - 1) // NCORES
    WDTL = (KI + NCORES - 1) // NCORES
    WOUTL = (KD + NCORES - 1) // NCORES
    x_in = din("x_fm", [D, tok])
    wfc_l = din("wfc_l", [FCL, 128, SFC])
    wpjA_l = din("wpjA_l", [PJL, 128, SPJH])
    wpjB_l = din("wpjB_l", [PJL, 128, SPJ - SPJH])
    win_f = din("win_f", [2 * KI, 128, D])
    wdt_l = din("wdt_l", [WDTL, 128, INNER])
    wout_l = din("wout_l", [WOUTL, 128, INNER])
    afc_in = din("afc", [128, KD * R])
    apj_in = din("apj", [128, KF * R])
    biasfc_in = din("bias_fc", [128, KF])
    biaspj_in = din("bias_pj", [128, KD])
    bdt_in = din("bias_dt", [128, KI])
    wnorm_in = din("wnorm", [128, KD])
    ident_in = din("ident", [128, 128])
    cmask_in = din("cmask", [128, NCORES])

    out_fm = nc.dram_tensor("out_fm", [D, tok], F32, kind="ExternalOutput").ap()

    FCC = FCL * QFC
    RO_PJA = 0
    RO_PJB = RO_PJA + PJL * SPJH
    PJC = RO_PJB + PJL * QPJB
    RO_WDT = 0
    RO_WOUT = RO_WDT + WDTL * 2 * INNER
    MXCB = RO_WOUT + WOUTL * 2 * INNER
    prep_fc = nc.dram_tensor("prep_fc", [128, FCC], BF16).ap()
    prep_fc_all = nc.dram_tensor("prep_fc_all", [NCORES, 128, FCC],
                                 BF16).ap()
    prep_pj = nc.dram_tensor("prep_pj", [128, PJC], BF16).ap()
    prep_pj_all = nc.dram_tensor("prep_pj_all", [NCORES, 128, PJC],
                                 BF16).ap()
    win_cache = nc.dram_tensor("win_cache", [128, 2 * KI * 2 * D],
                               BF16).ap()
    prep_mxB = nc.dram_tensor("prep_mxB", [128, MXCB], BF16).ap()
    prep_mxB_all = nc.dram_tensor("prep_mxB_all", [NCORES, 128, MXCB],
                                  BF16).ap()
    mxin = nc.dram_tensor("mxin", [128, 2], F32).ap()
    mxout = nc.dram_tensor("mxout", [128, 2], F32).ap()
    sv_spill = nc.dram_tensor("sv_spill", [nch, 128, KI * chunk], F32).ap()
    g_spill = nc.dram_tensor("g_spill", [nch, 128, KI * chunk], F32).ap()
    x1_spill = nc.dram_tensor("x1_spill", [nch, 128, KD * chunk], F32).ap()
    xin_spill = nc.dram_tensor("xin_spill", [nch, 128, KD * chunk], F32).ap()
    agin = nc.dram_tensor("agin", [2, 128, KI], F32).ap()
    agout = nc.dram_tensor("agout", [NCORES, 2, 128, KI], F32).ap()

    sb = {}

    def arena(name, parts, cols, dt=F32):
        h = nc.alloc_sbuf_tensor(name, [parts, cols], dt)
        sb[name] = h.ap()
        return h

    BIGN = max(KF, 2 * KI) * chunk
    arena("A_x", 128, KD * chunk)
    h_xn = arena("A_xn", 128, KD * chunk)
    arena("A_xin", 128, KD * chunk)
    h_big = arena("A_big", 128, BIGN)
    h_lam = arena("A_lam", 128, KI * chunk)
    arena("A_scr", 128, 4 * chunk)
    arena("A_st", 128, 8 * chunk)
    h_wa = arena("A_wa", 128, 2304)
    h_wb = arena("A_wb", 128, 2304)
    arena("A_qi", 128, min(1024, 2 * chunk), I32)
    h_tt = arena("A_tt", 128, 2 * chunk)
    NLA = (KD + KF) * R
    h_lp = arena("A_lp", 128, NLA)
    arena("A_g1", 1, 144)

    def alias(name, base_h, byte_off, parts, elems, dt):
        addr = nc.lookup_mloc(base_h).addr + byte_off
        sb[name] = nc.alloc_sbuf_tensor_at(
            name, [parts, elems], dt, offset=addr).ap()

    alias("A_xnh", h_xn, 0, 128, KD * chunk, BF16)
    alias("A_xnl", h_xn, 2 * KD * chunk, 128, KD * chunk, BF16)
    alias("A_bgh", h_big, 0, 128, BIGN, BF16)
    alias("A_bgl", h_big, 2 * KF * chunk, 128, KF * chunk, BF16)
    alias("A_wah", h_wa, 0, 128, 4608, BF16)
    alias("A_wbh", h_wb, 0, 128, 4608, BF16)
    assert nc.lookup_mloc("A_wb").addr == nc.lookup_mloc("A_wa").addr + 9216
    PJT = SPJH + QPJB
    assert 4 * QFC * 2 <= 18432 and 2 * PJT * 2 <= 18432
    alias("A_wq", h_wa, 0, 128, 4 * QFC, BF16)
    alias("A_wp", h_wa, 0, 128, 2 * PJT, BF16)
    alias("A_tth", h_tt, 0, 128, 2 * chunk, BF16)
    alias("A_ttl", h_tt, 4 * chunk, 128, 2 * chunk, BF16)
    alias("A_lmh", h_lam, 0, 128, 2 * KI * chunk, BF16)
    alias("A_lph", h_lp, 0, 128, NLA, BF16)
    alias("A_lpl", h_lp, 2 * NLA, 128, NLA, BF16)

    ALIAS = {"A_xnh": ("A_xn", 0), "A_xnl": ("A_xn", 2 * KD * chunk),
             "A_lmh": ("A_lam", 0),
             "A_bgh": ("A_big", 0), "A_bgl": ("A_big", 2 * KF * chunk),
             "A_wah": ("A_wa", 0), "A_wbh": ("A_wa", 9216),
             "A_wq": ("A_wa", 0), "A_wp": ("A_wa", 0),
             "A_tth": ("A_tt", 0), "A_ttl": ("A_tt", 4 * chunk),
             "A_lph": ("A_lp", 0), "A_lpl": ("A_lp", 2 * NLA)}

    C_AFC = 0
    C_APJ = C_AFC + KD * R
    C_BIASFC = C_APJ + KF * R
    C_BIASPJ = C_BIASFC + KF
    C_BDT = C_BIASPJ + KD
    C_WNORM = C_BDT + KI
    C_IDENT = C_WNORM + KD
    C_CMASK = C_IDENT + 128
    C_ONES = C_CMASK + NCORES
    C_ONESC = C_ONES + 128
    C_SCAL = C_ONESC + chunk
    C_MST = C_SCAL + 16
    C_SLAST = C_MST + 56
    C_RLAST = C_SLAST + KI
    C_CARRY = C_RLAST + KI
    C_AG = C_CARRY + KI
    C_BG = C_AG + NCORES * KI
    C_CAR = C_BG + NCORES * KI
    C_BPH = C_CAR + 3 * KI
    C_EPS = C_BPH + KD
    C_INVD = C_EPS + 1
    C_END = C_INVD + 1
    arena("A_c", 128, C_END)

    ps = [nc.alloc_psum_tensor(f"ps{i}", [128, 512], F32).ap()
          for i in range(8)]
    PS_MAIN = (0, 1, 2, 7)
    PS_S1, PS_S2, PS_T, PS_AUX = 3, 4, 5, 6

    p = Prog()

    def Bf(name, p0, p1, e0, e1):
        if name == "A_wb":
            return Buf("A_wa", p0, p1, 9216 + 4 * e0, 9216 + 4 * e1)
        return Buf(name, p0, p1, 4 * e0, 4 * e1)

    def BC(c0, n):
        return Buf("A_c", 0, 128, 4 * c0, 4 * (c0 + n))

    def BT(arena_name, t, width=None):
        w = chunk if width is None else width
        return Buf(arena_name, 0, 128, 4 * t * w, 4 * (t + 1) * w)

    def BA(alias_name, p0, p1, e0, e1):
        base, off = ALIAS[alias_name]
        return Buf(base, p0, p1, off + 2 * e0, off + 2 * e1)

    def BAT(alias_name, t, width=None):
        w = chunk if width is None else width
        return BA(alias_name, 0, 128, t * w, (t + 1) * w)

    def BD(name, lo, hi):
        return Buf("dram:" + name, 0, 1, lo, hi)

    def BP(i):
        return Buf("psum", 0, 128, i * 512, (i + 1) * 512)

    def dma(eng, out_ap, in_ap, reads, writes, tag=""):
        def fn(e, out_ap=out_ap, in_ap=in_ap):
            return e.dma_start(out=out_ap, in_=in_ap)
        p.op(eng, fn, reads=reads, writes=writes, dma=True, tag=tag)

    def act(fn_builder, reads, writes, tag=""):
        p.op("act", fn_builder, reads=reads, writes=writes, tag=tag)

    def dve(fn_builder, reads, writes, tag=""):
        p.op("dve", fn_builder, reads=reads, writes=writes, tag=tag)

    def pe(fn_builder, reads, writes, tag=""):
        p.op("pe", fn_builder, reads=reads, writes=writes, tag=tag)

    def pool(fn_builder, reads, writes, tag=""):
        p.op("pool", fn_builder, reads=reads, writes=writes, tag=tag)

    for (c0, n, src) in ((C_AFC, KD * R, afc_in), (C_APJ, KF * R, apj_in),
                         (C_BIASFC, KF, biasfc_in), (C_BIASPJ, KD, biaspj_in),
                         (C_BDT, KI, bdt_in), (C_WNORM, KD, wnorm_in),
                         (C_IDENT, 128, ident_in), (C_CMASK, NCORES, cmask_in)):
        dma("act", sb["A_c"][:, c0:c0 + n], src[:, :], [], [BC(c0, n)])

    dve(lambda e: e.memset(sb["A_c"][:, C_ONES:C_ONES + 128], 1.0),
        [], [BC(C_ONES, 128)], "ones")
    dve(lambda e: e.memset(sb["A_c"][:, C_ONESC:C_ONESC + chunk], 1.0),
        [], [BC(C_ONESC, chunk)], "onesc")
    dve(lambda e: e.memset(sb["A_c"][:, C_SLAST:C_SLAST + KI], 0.0),
        [], [BC(C_SLAST, KI)], "slast0")
    dve(lambda e: e.memset(sb["A_c"][:, C_RLAST:C_RLAST + KI], 1.0),
        [], [BC(C_RLAST, KI)], "rlast1")
    dve(lambda e: e.memset(sb["A_tth"][:, :], 0.0),
        [], [BA("A_tth", 0, 128, 0, 2 * chunk)], "tth0")
    dve(lambda e: e.memset(sb["A_ttl"][:, :], 0.0),
        [], [BA("A_ttl", 0, 128, 0, 2 * chunk)], "ttl0")
    dve(lambda e: e.memset(sb["A_c"][:, C_EPS:C_EPS + 1], EPS),
        [], [BC(C_EPS, 1)], "eps")
    dve(lambda e: e.memset(sb["A_c"][:, C_INVD:C_INVD + 1], 1.0 / D),
        [], [BC(C_INVD, 1)], "invd")
    dve(lambda e: e.tensor_scalar_mul(
        sb["A_c"][:, C_BPH:C_BPH + KD],
        sb["A_c"][:, C_BIASPJ:C_BIASPJ + KD], 0.5),
        [BC(C_BIASPJ, KD)], [BC(C_BPH, KD)], "bph")
    act(lambda e: e.activation(out=sb["A_lph"][:, 0:NLA],
                               in_=sb["A_c"][:, C_AFC:C_AFC + NLA],
                               func=AF.Copy),
        [BC(C_AFC, NLA)], [BA("A_lph", 0, 128, 0, NLA)], "lah")
    dve(lambda e: e.tensor_sub(sb["A_lpl"][:, 0:NLA],
                               sb["A_c"][:, C_AFC:C_AFC + NLA],
                               sb["A_lph"][:, 0:NLA]),
        [BC(C_AFC, NLA), BA("A_lph", 0, 128, 0, NLA)],
        [BA("A_lpl", 0, 128, 0, NLA)], "lal")

    MXW = max(D, SPJH, FF - SPJH)
    MXN = max(1, min(3, BIGN // MXW))

    def finish_scale(mxcol_rel, scal_rs, extra, tag):
        g1 = sb["A_g1"]
        mxcol = C_MST + mxcol_rel
        pe(lambda e: e.transpose(
            ps[PS_AUX][0:1, 0:128], sb["A_c"][:, mxcol:mxcol + 1],
            sb["A_c"][:, C_IDENT:C_IDENT + 128]),
           [BC(mxcol, 1), BC(C_IDENT, 128)], [BP(PS_AUX)], f"tr_{tag}")
        act(lambda e: e.activation(out=g1[0:1, 0:128],
                                   in_=ps[PS_AUX][0:1, 0:128], func=AF.Copy),
            [BP(PS_AUX)], [Bf("A_g1", 0, 1, 0, 128)], f"trc_{tag}")
        dve(lambda e: e.reduce_max(g1[0:1, 128:129], g1[0:1, 0:128],
                                   axis=mybir.AxisListType.X),
            [Bf("A_g1", 0, 1, 0, 128)], [Bf("A_g1", 0, 1, 128, 129)],
            f"gm_{tag}")
        dve(lambda e: e.reciprocal(g1[0:1, 129:130], g1[0:1, 128:129]),
            [Bf("A_g1", 0, 1, 128, 129)], [Bf("A_g1", 0, 1, 129, 130)],
            f"rg_{tag}")
        dve(lambda e: e.tensor_scalar_mul(g1[0:1, 130:131],
                                          g1[0:1, 129:130], QMAX),
            [Bf("A_g1", 0, 1, 129, 130)], [Bf("A_g1", 0, 1, 130, 131)],
            f"rs_{tag}")
        dve(lambda e: e.tensor_scalar_mul(g1[0:1, 131:132],
                                          g1[0:1, 128:129], 1.0 / QMAX),
            [Bf("A_g1", 0, 1, 128, 129)], [Bf("A_g1", 0, 1, 131, 132)],
            f"s_{tag}")
        pe(lambda e: e.matmul(ps[PS_AUX][:, 0:2],
                              sb["A_c"][0:1, C_ONES:C_ONES + 128],
                              g1[0:1, 130:132], start=True, stop=True),
           [BC(C_ONES, 128), Bf("A_g1", 0, 1, 130, 132)],
           [BP(PS_AUX)], f"bc_{tag}")
        act(lambda e: e.activation(
            out=sb["A_c"][:, C_SCAL + scal_rs:C_SCAL + scal_rs + 2],
            in_=ps[PS_AUX][:, 0:2], func=AF.Copy),
            [BP(PS_AUX)], [BC(C_SCAL + scal_rs, 2)], f"bcc_{tag}")
        dve(lambda e: e.tensor_scalar_mul(
            sb["A_c"][:, C_SCAL + extra:C_SCAL + extra + 1],
            sb["A_c"][:, C_SCAL + scal_rs:C_SCAL + scal_rs + 1], 2.0),
            [BC(C_SCAL + scal_rs, 1)], [BC(C_SCAL + extra, 1)],
            f"x2_{tag}")

    QU = min(512, chunk)

    def quant_to_bf(wb, wbh, wcols, scal_rs, tag):
        rs_ap = sb["A_c"][:, C_SCAL + scal_rs:C_SCAL + scal_rs + 1]
        rs_buf = BC(C_SCAL + scal_rs, 1)
        QS = ((0, chunk, 3 * chunk), (4 * chunk, 6 * chunk, 7 * chunk))
        for o in range(0, wcols, QU):
            L = min(QU, wcols - o)
            s0, s1_, s2_ = QS[(o // QU) % 2]
            qbase = ((o // QU) % 2) * QU
            wsl = sb[wb][:, o:o + L]
            wreg = Bf(wb, 0, 128, o, o + L)
            qi = sb["A_qi"][:, qbase:qbase + L]
            qireg = Bf("A_qi", 0, 128, qbase, qbase + L)
            r0 = sb["A_st"][:, s0:s0 + L]
            r0reg = Bf("A_st", 0, 128, s0, s0 + L)
            fr = sb["A_st"][:, s1_:s1_ + L]
            frreg = Bf("A_st", 0, 128, s1_, s1_ + L)
            g2 = sb["A_st"][:, s2_:s2_ + L]
            g2reg = Bf("A_st", 0, 128, s2_, s2_ + L)
            qo = sb[wbh][:, o:o + L]
            qoreg = BA(wbh, 0, 128, o, o + L)
            act(lambda e, wsl=wsl, qi=qi: e.activation(
                out=qi, in_=wsl, func=AF.Copy, scale=rs_ap),
                [wreg, rs_buf], [qireg], f"qc_{tag}")
            act(lambda e, qi=qi, r0=r0: e.activation(
                out=r0, in_=qi, func=AF.Copy),
                [qireg], [r0reg], f"qr_{tag}")
            dve(lambda e, wsl=wsl, r0=r0, fr=fr:
                e.scalar_tensor_tensor(out=fr, in0=wsl, scalar=rs_ap,
                                       in1=r0, op0=ALU.mult,
                                       op1=ALU.subtract),
                [wreg, r0reg, rs_buf], [frreg], f"qf_{tag}")
            dve(lambda e, fr=fr, r0=r0, g2=g2: e.scalar_tensor_tensor(
                out=g2, in0=fr, scalar=-0.5, in1=r0,
                op0=ALU.is_lt, op1=ALU.subtract),
                [frreg, r0reg], [g2reg], f"qg_{tag}")
            dve(lambda e, fr=fr, g2=g2, r0=r0: e.scalar_tensor_tensor(
                out=r0, in0=fr, scalar=0.5, in1=g2,
                op0=ALU.is_gt, op1=ALU.subtract),
                [frreg, g2reg], [r0reg], f"qa_{tag}")
            dve(lambda e, r0=r0, qo=qo: e.tensor_scalar(
                out=qo, in0=r0, scalar1=QMAX, scalar2=-QMAX - 1.0,
                op0=ALU.min, op1=ALU.max),
                [r0reg], [qoreg], f"qk_{tag}")

    def lora_split(wb, wbh, raw_e0, dst_e0, tag):
        raw = sb[wb][:, raw_e0:raw_e0 + 128]
        rreg = Bf(wb, 0, 128, raw_e0, raw_e0 + 128)
        hi = sb[wbh][:, dst_e0:dst_e0 + 128]
        hireg = BA(wbh, 0, 128, dst_e0, dst_e0 + 128)
        lo = sb[wbh][:, dst_e0 + 128:dst_e0 + 256]
        loreg = BA(wbh, 0, 128, dst_e0 + 128, dst_e0 + 256)
        act(lambda e: e.activation(out=hi, in_=raw, func=AF.Copy),
            [rreg], [hireg], f"lh_{tag}")
        dve(lambda e: e.tensor_sub(lo, raw, hi),
            [rreg, hireg], [loreg], f"ll_{tag}")

    ST_A, ST_MSQ, ST_T1, ST_RSTD, ST_RMS, ST_T2 = (
        0, chunk, 2 * chunk, 3 * chunk, 4 * chunk, 5 * chunk)

    def st(off):
        return sb["A_st"][:, off:off + chunk]

    def streg(off):
        return Bf("A_st", 0, 128, off, off + chunk)

    def ln_stats(x_arena, with_rms, tag):
        ones_l = sb["A_c"][:, C_ONES:C_ONES + 128]
        xt = [sb[x_arena][:, d * chunk:(d + 1) * chunk] for d in range(KD)]
        xreg = [BT(x_arena, d) for d in range(KD)]

        def s1fn(e):
            last = None
            for d in range(KD):
                last = e.matmul(ps[PS_S1][:, 0:chunk], ones_l, xt[d],
                                start=(d == 0), stop=(d == KD - 1))
            return last
        pe(s1fn, [BC(C_ONES, 128)] + xreg, [BP(PS_S1)], f"s1_{tag}")
        kdh = KD // 2
        for half in range(2):
            for i in range(kdh):
                d = half * kdh + i
                act(lambda e, d=d, i=i: e.activation(
                    out=sb["A_st"][:, i * chunk:(i + 1) * chunk],
                    in_=xt[d], func=AF.Square),
                    [xreg[d]], [streg(i * chunk)], f"sq_{tag}{d}")

            def s2fn(e, half=half):
                last = None
                for i in range(kdh):
                    last = e.matmul(
                        ps[PS_S2][:, 0:chunk], ones_l,
                        sb["A_st"][:, i * chunk:(i + 1) * chunk],
                        start=(half == 0 and i == 0),
                        stop=(half == 1 and i == kdh - 1))
                return last
            pe(s2fn, [BC(C_ONES, 128)] + [streg(i * chunk)
                                          for i in range(kdh)],
               [BP(PS_S2)], f"s2_{tag}{half}")
        ln_stats_chain(with_rms, tag)

    def ln_stats_chain(with_rms, tag):
        act(lambda e: e.activation(out=st(ST_A), in_=ps[PS_S1][:, 0:chunk],
                                   func=AF.Copy,
                                   scale=sb["A_c"][:, C_INVD:C_INVD + 1]),
            [BP(PS_S1), BC(C_INVD, 1)], [streg(ST_A)], f"ma_{tag}")
        act(lambda e: e.activation(out=st(ST_MSQ), in_=ps[PS_S2][:, 0:chunk],
                                   func=AF.Copy,
                                   scale=sb["A_c"][:, C_INVD:C_INVD + 1]),
            [BP(PS_S2), BC(C_INVD, 1)], [streg(ST_MSQ)], f"mq_{tag}")
        act(lambda e: e.activation(out=st(ST_T1), in_=st(ST_A),
                                   func=AF.Square),
            [streg(ST_A)], [streg(ST_T1)], f"a2_{tag}")
        dve(lambda e: e.tensor_sub(st(ST_MSQ), st(ST_MSQ), st(ST_T1)),
            [streg(ST_MSQ), streg(ST_T1)], [streg(ST_MSQ)], f"vr_{tag}")
        act(lambda e: e.activation(out=st(ST_T1), in_=st(ST_MSQ),
                                   func=AF.Ln,
                                   bias=sb["A_c"][:, C_EPS:C_EPS + 1]),
            [streg(ST_MSQ), BC(C_EPS, 1)], [streg(ST_T1)], f"sg_{tag}")
        act(lambda e: e.activation(out=st(ST_RSTD), in_=st(ST_T1),
                                   func=AF.Exp, scale=-0.5),
            [streg(ST_T1)], [streg(ST_RSTD)], f"rs_{tag}")
        if with_rms:
            dve(lambda e: e.tensor_mul(st(ST_T1), st(ST_MSQ), st(ST_RSTD)),
                [streg(ST_MSQ), streg(ST_RSTD)], [streg(ST_T1)],
                f"r1_{tag}")
            dve(lambda e: e.tensor_mul(st(ST_T1), st(ST_T1), st(ST_RSTD)),
                [streg(ST_T1), streg(ST_RSTD)], [streg(ST_T1)],
                f"r2_{tag}")
            act(lambda e: e.activation(out=st(ST_T2), in_=st(ST_T1),
                                       func=AF.Ln,
                                       bias=sb["A_c"][:, C_EPS:C_EPS + 1]),
                [streg(ST_T1), BC(C_EPS, 1)], [streg(ST_T2)], f"r3_{tag}")
            act(lambda e: e.activation(out=st(ST_RMS), in_=st(ST_T2),
                                       func=AF.Exp, scale=-0.5),
                [streg(ST_T2)], [streg(ST_RMS)], f"r4_{tag}")

    def ln_stats_d(x_arena, t, tag):
        ones_l = sb["A_c"][:, C_ONES:C_ONES + 128]
        xt_t = sb[x_arena][:, t * chunk:(t + 1) * chunk]
        xr = BT(x_arena, t)
        sq = sb["A_st"][:, (t % 2) * chunk:(t % 2 + 1) * chunk]
        sqr = streg((t % 2) * chunk)
        act(lambda e, xt_t=xt_t, sq=sq: e.activation(
            out=sq, in_=xt_t, func=AF.Square),
            [xr], [sqr], f"sq_{tag}{t}")

        def stfn(e, t=t, xt_t=xt_t, sq=sq):
            e.matmul(ps[PS_S1][:, 0:chunk], ones_l, xt_t,
                     start=(t == 0), stop=(t == KD - 1))
            return e.matmul(ps[PS_S2][:, 0:chunk], ones_l, sq,
                            start=(t == 0), stop=(t == KD - 1))
        pe(stfn, [BC(C_ONES, 128), xr, sqr],
           [BP(PS_S1), BP(PS_S2)], f"st_{tag}{t}")

    def ln_apply(x_arena, out_arena, tag):
        for d in range(KD):
            xs = sb[x_arena][:, d * chunk:(d + 1) * chunk]
            os_ = sb[out_arena][:, d * chunk:(d + 1) * chunk]
            dve(lambda e, xs=xs, os_=os_: e.tensor_sub(os_, xs, st(ST_A)),
                [BT(x_arena, d), streg(ST_A)], [BT(out_arena, d)],
                f"la_{tag}{d}")
            dve(lambda e, os_=os_: e.tensor_mul(os_, os_, st(ST_RSTD)),
                [BT(out_arena, d), streg(ST_RSTD)], [BT(out_arena, d)],
                f"lb_{tag}{d}")

    def ln_apply_pair(x_arena, tag):
        for d in range(KD):
            xs = sb[x_arena][:, d * chunk:(d + 1) * chunk]
            t1 = sb["A_scr"][:, (d % 2) * chunk:(d % 2 + 1) * chunk]
            t1reg = BT("A_scr", d % 2)
            t2 = sb["A_scr"][:, (2 + d % 2) * chunk:(3 + d % 2) * chunk]
            t2reg = BT("A_scr", 2 + d % 2)
            hi = sb["A_xnh"][:, d * chunk:(d + 1) * chunk]
            hireg = BAT("A_xnh", d)
            lo = sb["A_xnl"][:, d * chunk:(d + 1) * chunk]
            loreg = BAT("A_xnl", d)
            pool(lambda e, xs=xs, t1=t1: e.tensor_sub(t1, xs, st(ST_A)),
                 [BT(x_arena, d), streg(ST_A)], [t1reg], f"pa_{tag}{d}")
            dve(lambda e, t1=t1, t2=t2: e.tensor_mul(t2, t1, st(ST_RSTD)),
                [t1reg, streg(ST_RSTD)], [t2reg], f"pb_{tag}{d}")
            act(lambda e, t2=t2, hi=hi: e.activation(
                out=hi, in_=t2, func=AF.Copy),
                [t2reg], [hireg], f"pc_{tag}{d}")
            dve(lambda e, t2=t2, hi=hi, lo=lo: e.tensor_sub(lo, t2, hi),
                [t2reg, hireg], [loreg], f"pd_{tag}{d}")

    def lp_h(e0, n):
        return sb["A_lph"][:, e0:e0 + n]

    def lp_l(e0, n):
        return sb["A_lpl"][:, e0:e0 + n]

    def ffn(out_evac, tag, bg=None, after_fc=None, stats_x=None):
        xnh = [sb["A_xnh"][:, k * chunk:(k + 1) * chunk] for k in range(KD)]
        xnl = [sb["A_xnl"][:, k * chunk:(k + 1) * chunk] for k in range(KD)]
        xnhreg = [BAT("A_xnh", k) for k in range(KD)]
        xnlreg = [BAT("A_xnl", k) for k in range(KD)]
        lpbufs = [BA("A_lph", 0, 128, 0, NLA), BA("A_lpl", 0, 128, 0, NLA)]

        for k in range(KD):
            def tfk(e, k=k):
                ah = lp_h(k * R, R)
                al = lp_l(k * R, R)
                e.matmul(ps[PS_T][0:R, 0:chunk], ah, xnh[k],
                         start=(k == 0), stop=False)
                e.matmul(ps[PS_T][0:R, 0:chunk], ah, xnl[k],
                         start=False, stop=False)
                return e.matmul(ps[PS_T][0:R, 0:chunk], al, xnh[k],
                                start=False, stop=(k == KD - 1))
            pe(tfk, lpbufs + [xnhreg[k], xnlreg[k]], [BP(PS_T)],
               f"tf_{tag}{k}")
        ttmp = sb["A_st"][0:R, ST_T2:ST_T2 + chunk]
        ttmpreg = Bf("A_st", 0, R, ST_T2, ST_T2 + chunk)
        act(lambda e: e.activation(
            out=ttmp, in_=ps[PS_T][0:R, 0:chunk], func=AF.Copy,
            scale=sb["A_c"][0:R, C_SCAL + 4:C_SCAL + 5]),
            [BP(PS_T), BC(C_SCAL + 4, 1)], [ttmpreg], f"te_{tag}")
        act(lambda e: e.activation(
            out=sb["A_tth"][0:R, 0:chunk], in_=ttmp, func=AF.Copy),
            [ttmpreg], [BA("A_tth", 0, R, 0, chunk)], f"th_{tag}")
        dve(lambda e: e.tensor_sub(
            sb["A_ttl"][0:R, 0:chunk], ttmp, sb["A_tth"][0:R, 0:chunk]),
            [ttmpreg, BA("A_tth", 0, R, 0, chunk)],
            [BA("A_ttl", 0, R, 0, chunk)], f"tl_{tag}")

        for m in range(KF):
            tb = (m % 4) * QFC
            qreg = BA("A_wq", 0, 128, tb, tb + QFC)
            dma("sp", sb["A_wq"][:, tb:tb + QFC],
                prep_fc_all[m % NCORES][:, (m // NCORES) * QFC:
                                        (m // NCORES + 1) * QFC],
                [BD("agfc", 0, 1)], [qreg],
                f"lfq{m}")
            bank = PS_MAIN[m % len(PS_MAIN)]

            def mmfn(e, tb=tb, bank=bank):
                last = None
                for k in range(KD):
                    qt = sb["A_wq"][:, tb + k * 128:tb + (k + 1) * 128]
                    last = e.matmul(ps[bank][:, 0:chunk], qt, xnh[k],
                                    start=(k == 0), stop=False)
                    last = e.matmul(ps[bank][:, 0:chunk], qt, xnl[k],
                                    start=False, stop=False)
                lh = sb["A_wq"][:, tb + D:tb + D + 128]
                ll = sb["A_wq"][:, tb + D + 128:tb + D + 256]
                e.matmul(ps[bank][:, 0:chunk], lh, sb["A_tth"][:, 0:chunk],
                         start=False, stop=False)
                e.matmul(ps[bank][:, 0:chunk], lh, sb["A_ttl"][:, 0:chunk],
                         start=False, stop=False)
                last = e.matmul(ps[bank][:, 0:chunk], ll,
                                sb["A_tth"][:, 0:chunk],
                                start=False, stop=True)
                return last
            pe(mmfn,
               [qreg, Bf("A_tt", 0, 128, 0, 2 * chunk)]
               + xnhreg + xnlreg,
               [BP(bank)], f"fc_{tag}{m}")
            if bg:
                bg.pop(0)()
            tsl = ST_T1 if m % 2 == 0 else ST_T2
            act(lambda e, bank=bank, m=m, tsl=tsl: e.activation(
                out=st(tsl), in_=ps[bank][:, 0:chunk], func=AF.Gelu,
                scale=sb["A_c"][:, C_SCAL + 1:C_SCAL + 2],
                bias=sb["A_c"][:, C_BIASFC + m:C_BIASFC + m + 1]),
                [BP(bank), BC(C_SCAL + 1, 1), BC(C_BIASFC + m, 1)],
                [streg(tsl)], f"fe_{tag}{m}")
            act(lambda e, m=m, tsl=tsl: e.activation(
                out=sb["A_bgh"][:, m * chunk:(m + 1) * chunk],
                in_=st(tsl), func=AF.Copy),
                [streg(tsl)], [BAT("A_bgh", m)], f"fh_{tag}{m}")
            pool(lambda e, m=m, tsl=tsl: e.tensor_sub(
                sb["A_bgl"][:, m * chunk:(m + 1) * chunk], st(tsl),
                sb["A_bgh"][:, m * chunk:(m + 1) * chunk]),
                 [streg(tsl), BAT("A_bgh", m)], [BAT("A_bgl", m)],
                 f"fl_{tag}{m}")

        if bg:
            while bg:
                bg.pop(0)()
        if after_fc is not None:
            after_fc()
        h1h = [sb["A_bgh"][:, k * chunk:(k + 1) * chunk] for k in range(KF)]
        h1l = [sb["A_bgl"][:, k * chunk:(k + 1) * chunk] for k in range(KF)]
        h1hreg = [BAT("A_bgh", k) for k in range(KF)]
        h1lreg = [BAT("A_bgl", k) for k in range(KF)]

        def t2fn(e):
            last = None
            for k in range(KF):
                ah = lp_h(KD * R + k * R, R)
                al = lp_l(KD * R + k * R, R)
                last = e.matmul(ps[PS_T][0:R, 0:chunk], ah, h1h[k],
                                start=(k == 0), stop=False)
                last = e.matmul(ps[PS_T][0:R, 0:chunk], ah, h1l[k],
                                start=False, stop=False)
                last = e.matmul(ps[PS_T][0:R, 0:chunk], al, h1h[k],
                                start=False, stop=(k == KF - 1))
            return last
        pe(t2fn, lpbufs + h1hreg + h1lreg, [BP(PS_T)], f"t2_{tag}")
        act(lambda e: e.activation(
            out=ttmp, in_=ps[PS_T][0:R, 0:chunk], func=AF.Copy,
            scale=sb["A_c"][0:R, C_SCAL + 5:C_SCAL + 6]),
            [BP(PS_T), BC(C_SCAL + 5, 1)], [ttmpreg], f"t2e_{tag}")
        act(lambda e: e.activation(
            out=sb["A_tth"][0:R, chunk:2 * chunk], in_=ttmp, func=AF.Copy),
            [ttmpreg], [BA("A_tth", 0, R, chunk, 2 * chunk)], f"t2h_{tag}")
        dve(lambda e: e.tensor_sub(
            sb["A_ttl"][0:R, chunk:2 * chunk], ttmp,
            sb["A_tth"][0:R, chunk:2 * chunk]),
            [ttmpreg, BA("A_tth", 0, R, chunk, 2 * chunk)],
            [BA("A_ttl", 0, R, chunk, 2 * chunk)], f"t2l_{tag}")

        def stats_d(t):
            ones_l = sb["A_c"][:, C_ONES:C_ONES + 128]
            xt_t = sb[stats_x][:, t * chunk:(t + 1) * chunk]
            xr = BT(stats_x, t)
            sq = sb["A_st"][:, (t % 2) * chunk:(t % 2 + 1) * chunk]
            sqr = streg((t % 2) * chunk)
            act(lambda e, xt_t=xt_t, sq=sq: e.activation(
                out=sq, in_=xt_t, func=AF.Square),
                [xr], [sqr], f"sq_{tag}{t}")

            def stfn(e, t=t, xt_t=xt_t, sq=sq):
                e.matmul(ps[PS_S1][:, 0:chunk], ones_l, xt_t,
                         start=(t == 0), stop=(t == KD - 1))
                return e.matmul(ps[PS_S2][:, 0:chunk], ones_l, sq,
                                start=(t == 0), stop=(t == KD - 1))
            pe(stfn, [BC(C_ONES, 128), xr, sqr],
               [BP(PS_S1), BP(PS_S2)], f"st_{tag}{t}")

        for d in range(KD):
            pb_ = (d % 2) * PJT
            aregA = BA("A_wp", 0, 128, pb_, pb_ + SPJH)
            aregB = BA("A_wp", 0, 128, pb_ + SPJH, pb_ + PJT)
            dma("sp", sb["A_wp"][:, pb_:pb_ + SPJH],
                prep_pj_all[d % NCORES][:, RO_PJA + (d // NCORES) * SPJH:
                                        RO_PJA + (d // NCORES + 1) * SPJH],
                [BD("agpj", 0, 1)],
                [aregA], f"lqA{d}")
            dma("sp", sb["A_wp"][:, pb_ + SPJH:pb_ + PJT],
                prep_pj_all[d % NCORES][:, RO_PJB + (d // NCORES) * QPJB:
                                        RO_PJB + (d // NCORES + 1) * QPJB],
                [BD("agpj", 0, 1)],
                [aregB], f"lqB{d}")
            bank = PS_MAIN[d % len(PS_MAIN)]

            def pjA(e, bank=bank, pb_=pb_):
                last = None
                for k in range(KF2):
                    qt = sb["A_wp"][:, pb_ + k * 128:pb_ + (k + 1) * 128]
                    last = e.matmul(ps[bank][:, 0:chunk], qt, h1h[k],
                                    start=(k == 0), stop=False)
                    last = e.matmul(ps[bank][:, 0:chunk], qt, h1l[k],
                                    start=False, stop=False)
                return last
            pe(pjA, [aregA]
               + h1hreg[:KF2] + h1lreg[:KF2],
               [BP(bank)], f"pA_{tag}{d}")

            def pjB(e, bank=bank, pb_=pb_):
                last = None
                b0 = pb_ + SPJH
                for k in range(KF2):
                    qt = sb["A_wp"][:, b0 + k * 128:b0 + (k + 1) * 128]
                    last = e.matmul(ps[bank][:, 0:chunk], qt, h1h[KF2 + k],
                                    start=False, stop=False)
                    last = e.matmul(ps[bank][:, 0:chunk], qt, h1l[KF2 + k],
                                    start=False, stop=False)
                lh = sb["A_wp"][:, b0 + SPJH:b0 + SPJH + 128]
                ll = sb["A_wp"][:, b0 + SPJH + 128:b0 + SPJH + 256]
                th = sb["A_tth"][:, chunk:2 * chunk]
                tl = sb["A_ttl"][:, chunk:2 * chunk]
                e.matmul(ps[bank][:, 0:chunk], lh, th,
                         start=False, stop=False)
                e.matmul(ps[bank][:, 0:chunk], lh, tl,
                         start=False, stop=False)
                last = e.matmul(ps[bank][:, 0:chunk], ll, th,
                                start=False, stop=True)
                return last
            pe(pjB, [aregB,
                     Bf("A_tt", 0, 128, 0, 2 * chunk)]
               + h1hreg[KF2:] + h1lreg[KF2:],
               [BP(bank)], f"pB_{tag}{d}")
            out_evac(d, bank)
            if stats_x is not None and d >= 1:
                stats_d(d - 1)
        if stats_x is not None:
            stats_d(KD - 1)

    def prep_raw(dst_cols, src_ap, dram_buf, wb, tag):
        dma("sp", sb[wb][:, 0:dst_cols], src_ap,
            [dram_buf], [Bf(wb, 0, 128, 0, dst_cols)], f"pr_{tag}")

    def prep_maxabs(wcols, mst_col, wb, tag):
        dve(lambda e: e.reduce_max(
            sb["A_c"][:, C_MST + mst_col:C_MST + mst_col + 1],
            sb[wb][:, 0:wcols], axis=mybir.AxisListType.X,
            apply_absolute_value=True),
            [Bf(wb, 0, 128, 0, wcols)],
            [BC(C_MST + mst_col, 1)], f"mx_{tag}")

    for lp in range(FCL):
        wb = "A_wa" if lp % 2 == 0 else "A_wb"
        prep_raw(D, wfc_l[lp][:, 0:D], BD("wfcl", lp, lp + 1), wb, f"mf{lp}")
        prep_maxabs(D, lp, wb, f"mf{lp}")
    dve(lambda e: e.reduce_max(
        sb["A_c"][:, C_MST + 8:C_MST + 9],
        sb["A_c"][:, C_MST:C_MST + FCL], axis=mybir.AxisListType.X),
        [BC(C_MST, FCL)], [BC(C_MST + 8, 1)], "mxcf")
    for lp in range(PJL):
        prep_raw(SPJH, wpjA_l[lp], BD("wpjal", lp, lp + 1), "A_wa",
                 f"ma{lp}")
        prep_maxabs(SPJH, 10 + 2 * lp, "A_wa", f"ma{lp}")
        prep_raw(SPJH, wpjB_l[lp][:, 0:SPJH], BD("wpjbl", lp, lp + 1),
                 "A_wb", f"mb{lp}")
        prep_maxabs(SPJH, 11 + 2 * lp, "A_wb", f"mb{lp}")
    dve(lambda e: e.reduce_max(
        sb["A_c"][:, C_MST + 9:C_MST + 10],
        sb["A_c"][:, C_MST + 10:C_MST + 10 + 2 * PJL],
        axis=mybir.AxisListType.X),
        [BC(C_MST + 10, 2 * PJL)], [BC(C_MST + 9, 1)], "mxcp")
    dma("pool", mxin[:, :], sb["A_c"][:, C_MST + 8:C_MST + 10],
        [BC(C_MST + 8, 2)], [BD("mxin", 0, 1)], "mxs")

    def arfn(e):
        return e.collective_compute(
            "AllReduce", ALU.max, replica_groups=[list(range(NCORES))],
            ins=[mxin.opt()], outs=[mxout.opt()])
    p.op("pool", arfn, reads=[BD("mxin", 0, 1)], writes=[BD("mxout", 0, 1)],
         tag="ar")

    def load_x(ck):
        for d in range(KD):
            dma("act", sb["A_x"][:, d * chunk:(d + 1) * chunk],
                x_in[d * 128:(d + 1) * 128, ck * chunk:(ck + 1) * chunk],
                [BD("x_in", 0, 1)], [BT("A_x", d)], f"lx{ck}_{d}")
    load_x(0)
    ln_stats("A_x", False, "A0n1")
    ln_apply_pair("A_x", "A0n1")

    dma("pool", sb["A_c"][:, C_MST + 32:C_MST + 34], mxout[:, :],
        [BD("mxout", 0, 1)], [BC(C_MST + 32, 2)], "mxl")
    finish_scale(32, 0, 4, "fc")
    finish_scale(33, 2, 5, "pj")
    dve(lambda e: e.tensor_scalar_mul(
        sb["A_c"][:, C_SCAL + 6:C_SCAL + 7],
        sb["A_c"][:, C_SCAL + 3:C_SCAL + 4], 0.5),
        [BC(C_SCAL + 3, 1)], [BC(C_SCAL + 6, 1)], "shalf")

    for lp in range(FCL):
        wb = "A_wa" if lp % 2 == 0 else "A_wb"
        wbh = wb + "h"
        prep_raw(SFC, wfc_l[lp], BD("wfcl", lp, lp + 1), wb, f"qf{lp}")
        quant_to_bf(wb, wbh, D, 0, f"pf{lp}")
        lora_split(wb, wbh, D, D, f"pf{lp}")
        dma("pool", prep_fc[:, lp * QFC:(lp + 1) * QFC], sb[wbh][:, 0:QFC],
            [BA(wbh, 0, 128, 0, QFC)], [BD("pfc", lp, lp + 1)], f"sf{lp}")

    def ag1fn(e):
        return e.collective_compute(
            "AllGather", ALU.bypass, replica_groups=[list(range(NCORES))],
            ins=[prep_fc.opt()], outs=[prep_fc_all.opt()])
    p.op("pool", ag1fn, reads=[BD("pfc", 0, FCL)],
         writes=[BD("agfc", 0, 1)], tag="ag1")

    LBIG = 4 * KI * chunk >= 6 * 4096
    P_WB, P_WH = ("A_lam", "A_lmh") if LBIG else ("A_wa", "A_wah")
    prep2 = []
    for lp in range(PJL):
        def pa(lp=lp):
            dma("act", sb[P_WB][:, 0:SPJH], wpjA_l[lp],
                [BD("wpjal", lp, lp + 1)],
                [Bf(P_WB, 0, 128, 0, SPJH)], f"qa{lp}")
            quant_to_bf(P_WB, P_WH, SPJH, 2, f"pa{lp}")
            dma("pool", prep_pj[:, RO_PJA + lp * SPJH:
                                RO_PJA + (lp + 1) * SPJH],
                sb[P_WH][:, 0:SPJH],
                [BA(P_WH, 0, 128, 0, SPJH)], [BD("ppj", 0, 1)],
                f"sa{lp}")
        prep2.append(pa)

        def pb(lp=lp):
            dma("act", sb[P_WB][:, 0:SPJ - SPJH], wpjB_l[lp],
                [BD("wpjbl", lp, lp + 1)],
                [Bf(P_WB, 0, 128, 0, SPJ - SPJH)], f"qb{lp}")
            quant_to_bf(P_WB, P_WH, SPJH, 2, f"pb{lp}")
            lora_split(P_WB, P_WH, SPJH, SPJH, f"pb{lp}")
            dma("pool", prep_pj[:, RO_PJB + lp * QPJB:
                                RO_PJB + (lp + 1) * QPJB],
                sb[P_WH][:, 0:QPJB],
                [BA(P_WH, 0, 128, 0, QPJB)], [BD("ppj", 0, 1)],
                f"sb{lp}")
        prep2.append(pb)

    def agpj():
        def agfn(e):
            return e.collective_compute(
                "AllGather", ALU.bypass,
                replica_groups=[list(range(NCORES))],
                ins=[prep_pj.opt()], outs=[prep_pj_all.opt()])
        p.op("pool", agfn, reads=[BD("ppj", 0, 1)],
             writes=[BD("agpj", 0, 1)], tag="agpj")
    prep2.append(agpj)

    def prep_split(src_strip, dram_buf, wcols, rest_off, tag,
                   pmx=None, pdep="pmx"):
        PC = min(1024, wcols)
        if LBIG:
            rwb, rwh = "A_lam", "A_lmh"
            RB = KI * chunk // 2
        else:
            rwb, rwh = "A_wb", "A_wbh"
            RB = 0
        HB = 2 * RB + 2 * PC if LBIG else 2048
        for pcn in range(wcols // PC):
            o = pcn * PC
            raw = sb[rwb][:, RB:RB + PC]
            rreg = Bf(rwb, 0, 128, RB, RB + PC)
            dma("act", raw, src_strip[:, o:o + PC], [dram_buf], [rreg],
                f"sr_{tag}{pcn}")
            hi = sb[rwh][:, HB:HB + PC]
            hireg = BA(rwh, 0, 128, HB, HB + PC)
            lo = sb[rwh][:, HB + PC:HB + 2 * PC]
            loreg = BA(rwh, 0, 128, HB + PC, HB + 2 * PC)
            act(lambda e, raw=raw, hi=hi: e.activation(
                out=hi, in_=raw, func=AF.Copy),
                [rreg], [hireg], f"sh_{tag}{pcn}")
            dve(lambda e, raw=raw, hi=hi, lo=lo: e.tensor_sub(lo, raw, hi),
                [rreg, hireg], [loreg], f"sl_{tag}{pcn}")
            dma("pool", pmx[:, rest_off + o:rest_off + o + PC], hi,
                [hireg], [BD(pdep, 0, 1)], f"s1_{tag}{pcn}")
            dma("pool",
                pmx[:, rest_off + wcols + o:rest_off + wcols + o + PC],
                lo, [loreg], [BD(pdep, 0, 1)], f"s2_{tag}{pcn}")

    for lp in range(WDTL):
        prep2.append(lambda lp=lp: prep_split(
            wdt_l[lp], BD("wdtl", lp, lp + 1), INNER,
            RO_WDT + lp * 2 * INNER, f"dt{lp}", prep_mxB, "pmxB"))
    for lp in range(WOUTL):
        prep2.append(lambda lp=lp: prep_split(
            wout_l[lp], BD("woutl", lp, lp + 1), INNER,
            RO_WOUT + lp * 2 * INNER, f"wo{lp}", prep_mxB, "pmxB"))

    def emit_ag2(which):
        def ag2fn(e):
            return e.collective_compute(
                "AllGather", ALU.bypass,
                replica_groups=[list(range(NCORES))],
                ins=[prep_mxB.opt()], outs=[prep_mxB_all.opt()])
        p.op("pool", ag2fn, reads=[BD("pmxB", 0, 1)],
             writes=[BD("agmxB", 0, 1)], tag="ag2B")

    for ck in range(nch):

        def evac1(d, bank, ck=ck):
            tsl = ST_T2 if d % 2 == 0 else ST_T1
            dve(lambda e, bank=bank, d=d, tsl=tsl: e.tensor_scalar(
                out=st(tsl), in0=ps[bank][:, 0:chunk],
                scalar1=sb["A_c"][:, C_SCAL + 6:C_SCAL + 7],
                scalar2=sb["A_c"][:, C_BPH + d:C_BPH + d + 1],
                op0=ALU.mult, op1=ALU.add),
                [BP(bank), BC(C_SCAL + 6, 1), BC(C_BPH + d, 1)],
                [streg(tsl)], f"e1_{ck}{d}")
            dve(lambda e, d=d, tsl=tsl: e.tensor_add(
                sb["A_x"][:, d * chunk:(d + 1) * chunk],
                sb["A_x"][:, d * chunk:(d + 1) * chunk], st(tsl)),
                [BT("A_x", d), streg(tsl)], [BT("A_x", d)],
                f"x1_{ck}{d}")
        ffn(evac1, f"A{ck}", bg=(prep2 if ck == 0 else None),
            after_fc=((lambda: emit_ag2("B")) if ck == 0 else None),
            stats_x="A_x")
        dma("pool", x1_spill[ck], sb["A_x"][:, 0:KD * chunk],
            [Bf("A_x", 0, 128, 0, KD * chunk)], [BD("x1sp", ck, ck + 1)],
            f"sx1{ck}")

        ln_stats_chain(True, f"A{ck}n2")
        for d in range(KD):
            xs = sb["A_x"][:, d * chunk:(d + 1) * chunk]
            os_ = sb["A_xin"][:, d * chunk:(d + 1) * chunk]
            dve(lambda e, xs=xs, os_=os_: e.tensor_sub(os_, xs, st(ST_A)),
                [BT("A_x", d), streg(ST_A)], [BT("A_xin", d)],
                f"la_A{ck}n2{d}")
            dve(lambda e, os_=os_: e.tensor_mul(os_, os_, st(ST_RSTD)),
                [BT("A_xin", d), streg(ST_RSTD)], [BT("A_xin", d)],
                f"lb_A{ck}n2{d}")
            t1 = sb["A_scr"][:, (d % 2) * chunk:(d % 2 + 1) * chunk]
            t1reg = BT("A_scr", d % 2)
            t2 = sb["A_scr"][:, (2 + d % 2) * chunk:(3 + d % 2) * chunk]
            t2reg = BT("A_scr", 2 + d % 2)
            hi = sb["A_xnh"][:, d * chunk:(d + 1) * chunk]
            hireg = BAT("A_xnh", d)
            lo = sb["A_xnl"][:, d * chunk:(d + 1) * chunk]
            loreg = BAT("A_xnl", d)
            wn_ap = sb["A_c"][:, C_WNORM + d:C_WNORM + d + 1]
            dve(lambda e, d=d, t1=t1: e.tensor_mul(
                t1, sb["A_xin"][:, d * chunk:(d + 1) * chunk], st(ST_RMS)),
                [BT("A_xin", d), streg(ST_RMS)], [t1reg], f"hm_{ck}{d}")
            act(lambda e, t1=t1, hi=hi, wn_ap=wn_ap: e.activation(
                out=hi, in_=t1, func=AF.Copy, scale=wn_ap),
                [t1reg, BC(C_WNORM + d, 1)], [hireg], f"hh_{ck}{d}")
            act(lambda e, t1=t1, t2=t2, wn_ap=wn_ap: e.activation(
                out=t2, in_=t1, func=AF.Copy, scale=wn_ap),
                [t1reg, BC(C_WNORM + d, 1)], [t2reg], f"hf_{ck}{d}")
            dve(lambda e, t2=t2, hi=hi, lo=lo: e.tensor_sub(lo, t2, hi),
                [t2reg, hireg], [loreg], f"hl_{ck}{d}")
        dma("pool", xin_spill[ck], sb["A_xin"][:, 0:KD * chunk],
            [Bf("A_xin", 0, 128, 0, KD * chunk)], [BD("xnsp", ck, ck + 1)],
            f"sxn{ck}")

        xnh = [sb["A_xnh"][:, k * chunk:(k + 1) * chunk] for k in range(KD)]
        xnl = [sb["A_xnl"][:, k * chunk:(k + 1) * chunk] for k in range(KD)]
        def win_split(m):
            wb = "A_wa" if m % 2 == 0 else "A_wb"
            wbh = wb + "h"
            raw = sb[wb][:, D:2 * D]
            rawreg = Bf(wb, 0, 128, D, 2 * D)
            dma("sp", raw, win_f[m], [BD("winf", 0, 1)], [rawreg],
                f"li0{m}")
            hi = sb[wbh][:, 0:D]
            hireg = BA(wbh, 0, 128, 0, D)
            lo = sb[wbh][:, D:2 * D]
            loreg = BA(wbh, 0, 128, D, 2 * D)
            act(lambda e, raw=raw, hi=hi: e.activation(
                out=hi, in_=raw, func=AF.Copy),
                [rawreg], [hireg], f"lih0{m}")
            dve(lambda e, raw=raw, hi=hi, lo=lo: e.tensor_sub(
                lo, raw, hi),
                [rawreg, hireg], [loreg], f"lil0{m}")
            dma("pool", win_cache[:, m * 2 * D:(m + 1) * 2 * D],
                sb[wbh][:, 0:2 * D], [BA(wbh, 0, 128, 0, 2 * D)],
                [BD("wic", m, m + 1)], f"lis0{m}")

        for m in range(2 * KI):
            wb = "A_wa" if m % 2 == 0 else "A_wb"
            wbh = wb + "h"
            wqreg = BA(wbh, 0, 128, 0, 2 * D)
            if ck == 0:
                if m == 0:
                    win_split(0)
                if m + 1 < 2 * KI:
                    win_split(m + 1)
            else:
                dma("sp", sb[wbh][:, 0:2 * D],
                    win_cache[:, m * 2 * D:(m + 1) * 2 * D],
                    [BD("wic", m, m + 1)],
                    [wqreg], f"li{ck}{m}")
            bank = PS_MAIN[m % len(PS_MAIN)]

            def wik(e, wbh=wbh, bank=bank, k=0, first=False, last_=False):
                wh = sb[wbh][:, k * 128:(k + 1) * 128]
                wl = sb[wbh][:, D + k * 128:D + (k + 1) * 128]
                e.matmul(ps[bank][:, 0:chunk], wh, xnh[k],
                         start=first, stop=False)
                e.matmul(ps[bank][:, 0:chunk], wh, xnl[k],
                         start=False, stop=False)
                return e.matmul(ps[bank][:, 0:chunk], wl, xnh[k],
                                start=False, stop=last_)
            if m == 0:
                for k in range(KD):
                    def wik1(e, k=k, wbh=wbh, bank=bank):
                        return wik(e, wbh, bank, k, k == 0, k == KD - 1)
                    pe(wik1, [wqreg, BAT("A_xnh", k), BAT("A_xnl", k)],
                       [BP(bank)], f"wi_{ck}{m}_{k}")
            else:
                def wifn(e, wbh=wbh, bank=bank):
                    last = None
                    for k in range(KD):
                        last = wik(e, wbh, bank, k, k == 0, k == KD - 1)
                    return last
                pe(wifn, [wqreg] + [BAT("A_xnh", k) for k in range(KD)]
                   + [BAT("A_xnl", k) for k in range(KD)],
                   [BP(bank)], f"wi_{ck}{m}")
            if m < KI:
                tsl = ST_T1 if m % 2 == 0 else ST_T2
                act(lambda e, bank=bank, tsl=tsl: e.activation(
                    out=st(tsl), in_=ps[bank][:, 0:chunk], func=AF.Silu),
                    [BP(bank)], [streg(tsl)], f"we_{ck}{m}")
                act(lambda e, m=m, tsl=tsl: e.activation(
                    out=sb["A_bgh"][:, m * chunk:(m + 1) * chunk],
                    in_=st(tsl), func=AF.Copy),
                    [streg(tsl)], [BAT("A_bgh", m)], f"wh_{ck}{m}")
                dve(lambda e, m=m, tsl=tsl: e.tensor_sub(
                    sb["A_bgh"][:, (KI + m) * chunk:(KI + m + 1) * chunk],
                    st(tsl),
                    sb["A_bgh"][:, m * chunk:(m + 1) * chunk]),
                    [streg(tsl), BAT("A_bgh", m)], [BAT("A_bgh", KI + m)],
                    f"wl_{ck}{m}")
            else:
                j = m - KI
                act(lambda e, bank=bank, j=j: e.activation(
                    out=sb["A_big"][:, (KI + j) * chunk:(KI + j + 1) * chunk],
                    in_=ps[bank][:, 0:chunk], func=AF.Silu),
                    [BP(bank)], [BT("A_big", KI + j)], f"we_{ck}{m}")

        if ck + 1 < nch:
            load_x(ck + 1)
            ln_stats("A_x", False, f"A{ck + 1}n1")
            ln_apply_pair("A_x", f"A{ck + 1}n1")
        for j in range(KI):
            wb = "A_wa" if j % 2 == 0 else "A_wb"
            wbh = wb + "h"
            dma("sp", sb[wbh][:, 0:2 * INNER],
                prep_mxB_all[j % NCORES][
                    :, RO_WDT + (j // NCORES) * 2 * INNER:
                    RO_WDT + (j // NCORES + 1) * 2 * INNER],
                [BD("agmxB", 0, 1)],
                [BA(wbh, 0, 128, 0, 2 * INNER)], f"ld{ck}{j}")
            wqreg = BA(wbh, 0, 128, 0, 2 * INNER)
            bank = PS_MAIN[j % len(PS_MAIN)]

            def dtfn(e, wbh=wbh, bank=bank):
                last = None
                for k in range(KI):
                    wh = sb[wbh][:, k * 128:(k + 1) * 128]
                    wl = sb[wbh][:, INNER + k * 128:INNER + (k + 1) * 128]
                    uh = sb["A_bgh"][:, k * chunk:(k + 1) * chunk]
                    ul = sb["A_bgh"][:, (KI + k) * chunk:(KI + k + 1) * chunk]
                    e.matmul(ps[bank][:, 0:chunk], wh, uh,
                             start=(k == 0), stop=False)
                    e.matmul(ps[bank][:, 0:chunk], wh, ul,
                             start=False, stop=False)
                    last = e.matmul(ps[bank][:, 0:chunk], wl, uh,
                                    start=False, stop=(k == KI - 1))
                return last
            pe(dtfn, [wqreg] + [BAT("A_bgh", k) for k in range(2 * KI)],
               [BP(bank)], f"dt_{ck}{j}")
            act(lambda e, bank=bank, j=j: e.activation(
                out=sb["A_lam"][:, j * chunk:(j + 1) * chunk],
                in_=ps[bank][:, 0:chunk], func=AF.Sigmoid,
                bias=sb["A_c"][:, C_BDT + j:C_BDT + j + 1]),
                [BP(bank), BC(C_BDT + j, 1)], [BT("A_lam", j)],
                f"de_{ck}{j}")

            lam_j = sb["A_lam"][:, j * chunk:(j + 1) * chunk]
            lamreg = BT("A_lam", j)
            u_j = sb["A_scr"][:, 3 * chunk:4 * chunk]
            ureg = BT("A_scr", 3)
            sz_j = sb["A_big"][:, (KI + j) * chunk:(KI + j + 1) * chunk]
            szreg = BT("A_big", KI + j)
            dve(lambda e, u_j=u_j, j=j: e.tensor_add(
                u_j, sb["A_bgh"][:, j * chunk:(j + 1) * chunk],
                sb["A_bgh"][:, (KI + j) * chunk:(KI + j + 1) * chunk]),
                [BAT("A_bgh", j), BAT("A_bgh", KI + j)], [ureg],
                f"ur_{ck}{j}")
            b_j = sb["A_scr"][:, 0:chunk]
            breg = BT("A_scr", 0)
            s_j = sb["A_scr"][:, chunk:2 * chunk]
            sreg = BT("A_scr", 1)
            p_j = sb["A_scr"][:, 2 * chunk:3 * chunk]
            preg = BT("A_scr", 2)
            slast = sb["A_c"][:, C_SLAST + j:C_SLAST + j + 1]
            slreg = BC(C_SLAST + j, 1)
            rlast = sb["A_c"][:, C_RLAST + j:C_RLAST + j + 1]
            rlreg = BC(C_RLAST + j, 1)
            dve(lambda e, lam_j=lam_j, u_j=u_j, b_j=b_j: e.tensor_mul(
                b_j, lam_j, u_j), [lamreg, ureg], [breg], f"b1_{ck}{j}")
            dve(lambda e, u_j=u_j, b_j=b_j: e.tensor_sub(b_j, u_j, b_j),
                [ureg, breg], [breg], f"b2_{ck}{j}")
            dve(lambda e, lam_j=lam_j, b_j=b_j, s_j=s_j, slast=slast:
                e.tensor_tensor_scan(out=s_j, data0=lam_j, data1=b_j,
                                     initial=slast, op0=ALU.mult,
                                     op1=ALU.add),
                [lamreg, breg, slreg], [sreg], f"sc_{ck}{j}")
            dve(lambda e, lam_j=lam_j, p_j=p_j, rlast=rlast:
                e.tensor_tensor_scan(
                    out=p_j, data0=lam_j,
                    data1=sb["A_c"][:, C_ONESC:C_ONESC + chunk],
                    initial=rlast, op0=ALU.mult, op1=ALU.mult),
                [lamreg, BC(C_ONESC, chunk), rlreg], [preg], f"cp_{ck}{j}")
            dve(lambda e, s_j=s_j, slast=slast: e.tensor_copy(
                slast, s_j[:, chunk - 1:chunk]),
                [sreg], [slreg], f"sl_{ck}{j}")
            dve(lambda e, p_j=p_j, rlast=rlast: e.tensor_copy(
                rlast, p_j[:, chunk - 1:chunk]),
                [preg], [rlreg], f"rl_{ck}{j}")
            dve(lambda e, s_j=s_j, sz_j=sz_j: e.tensor_mul(s_j, s_j, sz_j),
                [sreg, szreg], [sreg], f"s3_{ck}{j}")
            dve(lambda e, p_j=p_j, sz_j=sz_j: e.tensor_mul(p_j, p_j, sz_j),
                [preg, szreg], [preg], f"g3_{ck}{j}")
            dma("pool", sv_spill[ck][:, j * chunk:(j + 1) * chunk], s_j,
                [sreg], [BD("svsp", ck * KI + j, ck * KI + j + 1)],
                f"ss_{ck}{j}")
            dma("pool", g_spill[ck][:, j * chunk:(j + 1) * chunk], p_j,
                [preg], [BD("gsp", ck * KI + j, ck * KI + j + 1)],
                f"sg_{ck}{j}")


    def g_ap(j):
        if j < KD:
            return (sb["A_xin"][:, j * chunk:(j + 1) * chunk],
                    BT("A_xin", j))
        return (sb["A_xn"][:, (j - KD) * chunk:(j - KD + 1) * chunk],
                BT("A_xn", j - KD))

    def prologue_b_loads(ck, with_x1=False):
        dma("act", sb["A_lam"][:, 0:KI * chunk], sv_spill[ck],
            [BD("svsp", ck * KI, (ck + 1) * KI)],
            [Bf("A_lam", 0, 128, 0, KI * chunk)], f"rv{ck}")
        gh = KD * chunk
        dma("act", sb["A_xin"][:, 0:gh], g_spill[ck][:, 0:gh],
            [BD("gsp", ck * KI, (ck + 1) * KI)],
            [Bf("A_xin", 0, 128, 0, gh)], f"rg{ck}a")
        dma("sp", sb["A_xn"][:, 0:gh], g_spill[ck][:, gh:2 * gh],
            [BD("gsp", ck * KI, (ck + 1) * KI)],
            [Bf("A_xn", 0, 128, 0, gh)], f"rg{ck}b")
        if with_x1:
            dma("act", sb["A_x"][:, 0:KD * chunk], x1_spill[ck],
                [BD("x1sp", ck, ck + 1)],
                [Bf("A_x", 0, 128, 0, KD * chunk)], f"r1{ck}")

    def prologue_b_corr(ck):
        for j in range(KI):
            gA, gR = g_ap(j)
            svf = sb["A_scr"][:, (j % 2) * chunk:(j % 2 + 1) * chunk]
            svfreg = BT("A_scr", j % 2)
            svh = sb["A_lmh"][:, 2 * j * chunk:2 * j * chunk + chunk]
            svhreg = BA("A_lmh", 0, 128, 2 * j * chunk,
                        2 * j * chunk + chunk)
            svl = sb["A_lmh"][:, 2 * j * chunk + chunk:2 * (j + 1) * chunk]
            svlreg = BA("A_lmh", 0, 128, 2 * j * chunk + chunk,
                        2 * (j + 1) * chunk)
            dve(lambda e, j=j, svf=svf, gA=gA: e.scalar_tensor_tensor(
                out=svf, in0=gA,
                scalar=sb["A_c"][:, C_CARRY + j:C_CARRY + j + 1],
                in1=sb["A_lam"][:, j * chunk:(j + 1) * chunk],
                op0=ALU.mult, op1=ALU.add),
                [gR, BC(C_CARRY + j, 1), BT("A_lam", j)],
                [svfreg], f"sv{ck}_{j}")
            act(lambda e, svf=svf, svh=svh: e.activation(
                out=svh, in_=svf, func=AF.Copy),
                [svfreg], [svhreg], f"svh{ck}_{j}")
            pool(lambda e, svf=svf, svh=svh, svl=svl: e.tensor_sub(
                svl, svf, svh),
                 [svfreg, svhreg], [svlreg], f"svl{ck}_{j}")
        dma("act", sb["A_xin"][:, 0:KD * chunk], xin_spill[ck],
            [BD("xnsp", ck, ck + 1)],
            [Bf("A_xin", 0, 128, 0, KD * chunk)], f"rn{ck}")

    def prologue_b(ck):
        prologue_b_loads(ck)
        prologue_b_corr(ck)

    prologue_b_loads(0, with_x1=True)
    dma("pool", agin[0], sb["A_c"][:, C_RLAST:C_RLAST + KI],
        [BC(C_RLAST, KI)], [BD("agin", 0, 1)], "agA")
    dma("pool", agin[1], sb["A_c"][:, C_SLAST:C_SLAST + KI],
        [BC(C_SLAST, KI)], [BD("agin", 1, 2)], "agB")

    def ccfn(e):
        return e.collective_compute(
            "AllGather", ALU.bypass, replica_groups=[list(range(NCORES))],
            ins=[agin.opt()], outs=[agout.opt()])
    p.op("pool", ccfn, reads=[BD("agin", 0, 2)], writes=[BD("agout", 0, 16)],
         tag="cc")
    for r in range(NCORES):
        dma("pool", sb["A_c"][:, C_AG + r * KI:C_AG + (r + 1) * KI],
            agout[r, 0], [BD("agout", 0, 16)], [BC(C_AG + r * KI, KI)],
            f"ga{r}")
        dma("pool", sb["A_c"][:, C_BG + r * KI:C_BG + (r + 1) * KI],
            agout[r, 1], [BD("agout", 0, 16)], [BC(C_BG + r * KI, KI)],
            f"gb{r}")

    def AGc(r):
        return sb["A_c"][:, C_AG + r * KI:C_AG + (r + 1) * KI]

    def BGc(r):
        return sb["A_c"][:, C_BG + r * KI:C_BG + (r + 1) * KI]

    carry = sb["A_c"][:, C_CARRY:C_CARRY + KI]
    carryreg = BC(C_CARRY, KI)
    dve(lambda e: e.memset(carry, 0.0), [], [carryreg], "c0")
    agrd = [BC(C_AG, NCORES * KI), BC(C_BG, NCORES * KI)]
    for b in range(2):
        base = 4 * b
        c1 = sb["A_c"][:, C_CAR:C_CAR + KI]
        c2 = sb["A_c"][:, C_CAR + KI:C_CAR + 2 * KI]
        c3 = sb["A_c"][:, C_CAR + 2 * KI:C_CAR + 3 * KI]
        r1 = BC(C_CAR, KI)
        r2 = BC(C_CAR + KI, KI)
        r3 = BC(C_CAR + 2 * KI, KI)
        dve(lambda e, c1=c1, base=base: e.tensor_copy(c1, BGc(base)),
            agrd, [r1], f"c1_{b}")
        dve(lambda e, c2=c2, c1=c1, base=base: e.tensor_mul(
            c2, AGc(base + 1), c1), agrd + [r1], [r2], f"c2a_{b}")
        dve(lambda e, c2=c2, base=base: e.tensor_add(
            c2, BGc(base + 1), c2), agrd + [r2], [r2], f"c2b_{b}")
        dve(lambda e, c3=c3, c2=c2, base=base: e.tensor_mul(
            c3, AGc(base + 2), c2), agrd + [r2], [r3], f"c3a_{b}")
        dve(lambda e, c3=c3, base=base: e.tensor_add(
            c3, BGc(base + 2), c3), agrd + [r3], [r3], f"c3b_{b}")
        for i, (cx, rx) in enumerate(((c1, r1), (c2, r2), (c3, r3))):
            col = C_CMASK + base + 1 + i
            dve(lambda e, cx=cx, col=col: e.scalar_tensor_tensor(
                out=carry, in0=cx, scalar=sb["A_c"][:, col:col + 1],
                in1=carry, op0=ALU.mult, op1=ALU.add),
                [rx, BC(col, 1), carryreg], [carryreg], f"se_{b}{i}")

    for ck in range(nch):
        if ck == 0:
            prologue_b_corr(0)
        else:
            dma("act", sb["A_x"][:, 0:KD * chunk], x1_spill[ck],
                [BD("x1sp", ck, ck + 1)],
                [Bf("A_x", 0, 128, 0, KD * chunk)], f"r1{ck}")
        for d in range(KD):
            wb = "A_wa" if d % 2 == 0 else "A_wb"
            wbh = wb + "h"
            dma("sp", sb[wbh][:, 0:2 * INNER],
                prep_mxB_all[d % NCORES][
                    :, RO_WOUT + (d // NCORES) * 2 * INNER:
                    RO_WOUT + (d // NCORES + 1) * 2 * INNER],
                [BD("agmxB", 0, 1)],
                [BA(wbh, 0, 128, 0, 2 * INNER)], f"lo{ck}{d}")
            wqreg = BA(wbh, 0, 128, 0, 2 * INNER)
            bank = PS_MAIN[d % len(PS_MAIN)]

            def wok(e, wbh=wbh, bank=bank, k=0):
                wh = sb[wbh][:, k * 128:(k + 1) * 128]
                wl = sb[wbh][:, INNER + k * 128:INNER + (k + 1) * 128]
                sh = sb["A_lmh"][:, 2 * k * chunk:2 * k * chunk + chunk]
                sl = sb["A_lmh"][:, 2 * k * chunk + chunk:
                                 2 * (k + 1) * chunk]
                e.matmul(ps[bank][:, 0:chunk], wh, sh,
                         start=(k == 0), stop=False)
                e.matmul(ps[bank][:, 0:chunk], wh, sl,
                         start=False, stop=False)
                return e.matmul(ps[bank][:, 0:chunk], wl, sh,
                                start=False, stop=(k == KI - 1))
            if d == 0:
                for k in range(KI):
                    def wok1(e, wbh=wbh, bank=bank, k=k):
                        return wok(e, wbh, bank, k)
                    pe(wok1, [wqreg,
                              BA("A_lmh", 0, 128, 2 * k * chunk,
                                 2 * (k + 1) * chunk)],
                       [BP(bank)], f"wo_{ck}{d}_{k}")
            else:
                def wofn(e, wbh=wbh, bank=bank):
                    last = None
                    for k in range(KI):
                        last = wok(e, wbh, bank, k)
                    return last
                pe(wofn, [wqreg, BA("A_lmh", 0, 128, 0, 2 * KI * chunk)],
                   [BP(bank)], f"wo_{ck}{d}")
            tsl = ST_T1 if d % 2 == 0 else ST_T2
            dve(lambda e, bank=bank, d=d, tsl=tsl: e.tensor_add(
                st(tsl), ps[bank][:, 0:chunk],
                sb["A_xin"][:, d * chunk:(d + 1) * chunk]),
                [BP(bank), BT("A_xin", d)], [streg(tsl)], f"y1{ck}{d}")
            dve(lambda e, d=d, tsl=tsl: e.tensor_add(
                sb["A_x"][:, d * chunk:(d + 1) * chunk], st(tsl),
                sb["A_x"][:, d * chunk:(d + 1) * chunk]),
                [streg(tsl), BT("A_x", d)], [BT("A_x", d)],
                f"y2{ck}{d}")
            if d >= 1:
                ln_stats_d("A_x", d - 1, f"B{ck}n3")
        ln_stats_d("A_x", KD - 1, f"B{ck}n3")
        ln_stats_chain(False, f"B{ck}n3")
        ln_apply_pair("A_x", f"B{ck}n3")

        def evac2(d, bank, ck=ck):
            tsl = ST_T2 if d % 2 == 0 else ST_T1
            dve(lambda e, bank=bank, d=d, tsl=tsl: e.tensor_scalar(
                out=st(tsl), in0=ps[bank][:, 0:chunk],
                scalar1=sb["A_c"][:, C_SCAL + 6:C_SCAL + 7],
                scalar2=sb["A_c"][:, C_BPH + d:C_BPH + d + 1],
                op0=ALU.mult, op1=ALU.add),
                [BP(bank), BC(C_SCAL + 6, 1), BC(C_BPH + d, 1)],
                [streg(tsl)], f"e2_{ck}{d}")
            dve(lambda e, d=d, tsl=tsl: e.tensor_add(
                sb["A_x"][:, d * chunk:(d + 1) * chunk],
                sb["A_x"][:, d * chunk:(d + 1) * chunk], st(tsl)),
                [BT("A_x", d), streg(tsl)], [BT("A_x", d)],
                f"o1_{ck}{d}")
            dma("pool",
                out_fm[d * 128:(d + 1) * 128, ck * chunk:(ck + 1) * chunk],
                sb["A_x"][:, d * chunk:(d + 1) * chunk],
                [BT("A_x", d)], [BD("out", ck * KD + d, ck * KD + d + 1)],
                f"so_{ck}{d}")
        ffn(evac2, f"B{ck}",
            after_fc=((lambda ck=ck: prologue_b(ck + 1))
                      if ck + 1 < nch else None))

    p.wait_all_end()

    with contextlib.ExitStack() as ctx:
        sems = {}
        for e in ENGINES:
            sems[("t", e)] = ctx.enter_context(nc.semaphore(f"t_{e}"))
        for e in DMA_ENGS:
            for k in range(QPOOL[e]):
                sems[("q", e, k)] = ctx.enter_context(
                    nc.semaphore(f"q_{e}{k}"))
        fns = p.replay_fns(sems)
        with nc.Block() as block:
            block.tensor(fns["pe"])
            block.scalar(fns["act"])
            block.vector(fns["dve"])
            block.gpsimd(fns["pool"])
            block.sync(fns["sp"])
    return nc



def pack_lhsT(W):
    out_dim, in_dim = W.shape
    S, Kc = out_dim // 128, in_dim // 128
    A = W.reshape(S, 128, Kc, 128).transpose(0, 3, 2, 1)
    return np.ascontiguousarray(A.reshape(S, 128, in_dim), dtype=np.float32)


def pack_lhsT_lora(W, Bl):
    S = W.shape[0] // 128
    base = pack_lhsT(W)
    lora = np.zeros((S, 128, 128), dtype=np.float32)
    for s in range(S):
        lora[s, 0:R, :] = Bl[s * 128:(s + 1) * 128, :].T
    return np.ascontiguousarray(np.concatenate([base, lora], axis=2))


def shard_strips(strips, nloc, core, c0=None, c1=None):
    n = strips.shape[0]
    cols = strips.shape[2] if c0 is None else c1 - c0
    out = np.zeros((nloc, 128, cols), dtype=np.float32)
    for lp in range(nloc):
        idx = core + NCORES * lp
        if idx < n:
            out[lp] = strips[idx] if c0 is None else strips[idx, :, c0:c1]
    return np.ascontiguousarray(out)


def build_inputs(full, tok):
    x = np.asarray(full["L0_attn_x"], dtype=np.float32)
    wfc = pack_lhsT_lora(np.asarray(full["L0_ffn_W_fc"], np.float32),
                         np.asarray(full["L0_ffn_B_fc"], np.float32))
    wpj = pack_lhsT_lora(np.asarray(full["L0_ffn_W_proj"], np.float32),
                         np.asarray(full["L0_ffn_B_proj"], np.float32))
    win = pack_lhsT(np.asarray(full["L0_attn_W_in"], np.float32))
    wdt = pack_lhsT(np.asarray(full["L0_attn_W_dt"], np.float32))
    wout = pack_lhsT(np.asarray(full["L0_attn_W_out"], np.float32))
    FCL = (KF + NCORES - 1) // NCORES
    PJL = (KD + NCORES - 1) // NCORES
    WINL = (2 * KI + NCORES - 1) // NCORES
    WDTL = (KI + NCORES - 1) // NCORES
    WOUTL = (KD + NCORES - 1) // NCORES
    afc = np.ascontiguousarray(
        np.asarray(full["L0_ffn_A_fc"], np.float32)
        .reshape(R, KD, 128).transpose(2, 1, 0).reshape(128, KD * R))
    apj = np.ascontiguousarray(
        np.asarray(full["L0_ffn_A_proj"], np.float32)
        .reshape(R, KF, 128).transpose(2, 1, 0).reshape(128, KF * R))
    bias_fc = np.ascontiguousarray(
        np.asarray(full["L0_ffn_b_fc"], np.float32).reshape(KF, 128).T)
    bias_pj = np.ascontiguousarray(
        np.asarray(full["L0_ffn_b_proj"], np.float32).reshape(KD, 128).T)
    bias_dt = np.ascontiguousarray(
        np.asarray(full["L0_attn_b_dt"], np.float32).reshape(KI, 128).T)
    wnorm = np.ascontiguousarray(
        np.asarray(full["L0_attn_w_norm"], np.float32).reshape(KD, 128).T)
    ident = np.eye(128, dtype=np.float32)

    ins = []
    for c in range(NCORES):
        bc, pos = c // 4, c % 4
        t0 = pos * tok
        x_fm = np.ascontiguousarray(x[bc, t0:t0 + tok, :].T)
        cmask = np.zeros((128, NCORES), dtype=np.float32)
        cmask[:, c] = 1.0
        ins.append(dict(
            x_fm=x_fm,
            wfc_l=shard_strips(wfc, FCL, c),
            wpjA_l=shard_strips(wpj, PJL, c, 0, SPJH),
            wpjB_l=shard_strips(wpj, PJL, c, SPJH, SPJ),
            win_f=np.ascontiguousarray(win),
            wdt_l=shard_strips(wdt, WDTL, c),
            wout_l=shard_strips(wout, WOUTL, c),
            afc=afc, apj=apj,
            bias_fc=bias_fc, bias_pj=bias_pj, bias_dt=bias_dt,
            wnorm=wnorm, ident=ident, cmask=cmask))
    return ins


def kernel(**inputs):
    tok, chunk = 2048, 512
    ins = build_inputs(inputs, tok)
    nc = build_program(tok, chunk)
    res = run_bass_kernel_spmd(nc, ins, core_ids=list(range(NCORES)))
    out = np.empty((2, 8192, D), dtype=np.float32)
    for c in range(NCORES):
        bc, pos = c // 4, c % 4
        t0 = pos * tok
        out[bc, t0:t0 + tok, :] = res.results[c]["out_fm"].T
    return out
